# version 02
# baseline (speedup 1.0000x reference)
import math
from contextlib import ExitStack

import numpy as np
import ml_dtypes

import concourse.bass as bass
import concourse.mybir as mybir
from concourse.bass_utils import run_bass_kernel_spmd

F32 = mybir.dt.float32
BF16 = mybir.dt.bfloat16
AF = mybir.ActivationFunctionType
ALU = mybir.AluOpType
AX = mybir.AxisListType

T = 4096
DM = 1024
DI = 2048
DST = 16
DTR = 64
NE = 8
HF = 4096
NH = 16
DKH = 64
CH = 512
NCH = T // CH
LNEPS = 1e-5

_PHASES = ("setup", "mamba", "ffn1", "attn", "ffn2")
MSTAGE = 99
MCHUNKS = NCH

ENGINES = ("tensor", "vector", "scalar", "gpsimd", "sync")
QUEUES = {"qsync": "sync", "qsca": "scalar", "qgp": "gpsimd"}
QLANES = 6
LANE_DOMAINS = tuple(f"{q}{l}" for q in QUEUES for l in range(QLANES))
DOMAINS = ENGINES + LANE_DOMAINS


class _Op:
    __slots__ = ("eng", "domain", "fn", "deps", "inc_needed", "inc_val",
                 "waits", "clock", "tag")

    def __init__(self, eng, domain, fn, tag):
        self.eng = eng
        self.domain = domain
        self.fn = fn
        self.deps = []
        self.inc_needed = False
        self.inc_val = 0
        self.waits = []
        self.clock = None
        self.tag = tag


class Builder:
    def __init__(self, nc):
        self.nc = nc
        self.ops = []
        self.by_eng = {e: [] for e in ENGINES}
        self.last_in_domain = {d: None for d in DOMAINS}
        self.last_write = {}
        self.readers = {}
        self.same_engine_edges = {"tensor": False, "vector": True,
                                  "scalar": True, "gpsimd": True, "sync": False}
        self.sems = {}
        self.qcount = {q: 0 for q in QUEUES}

    def _dep(self, op, producer, kind="raw"):
        if producer is None or producer is op:
            return
        if producer.eng == op.eng and producer.domain in ENGINES:
            if kind in ("war", "waw"):
                return
            if kind == "raw" and not self.same_engine_edges[op.eng]:
                return
        if producer not in op.deps:
            op.deps.append(producer)

    def _record(self, op, reads, writes, sreads):
        for k in reads:
            self._dep(op, self.last_write.get(k), "raw")
        for k in sreads:
            self._dep(op, self.last_write.get(k), "sraw")
        for k in writes:
            self._dep(op, self.last_write.get(k), "waw")
            for r in self.readers.get(k, ()):
                self._dep(op, r, "war")
        for k in reads:
            self.readers.setdefault(k, []).append(op)
        for k in sreads:
            self.readers.setdefault(k, []).append(op)
        for k in writes:
            self.last_write[k] = op
            self.readers[k] = []
        self.ops.append(op)
        self.by_eng[op.eng].append(op)
        self.last_in_domain[op.domain] = op
        return op

    def op(self, eng, fn, reads=(), writes=(), sreads=(), tag=""):
        return self._record(_Op(eng, eng, fn, tag), reads, writes, sreads)

    def dma(self, queue, fn, reads=(), writes=(), tag=""):
        lane = self.qcount[queue] % QLANES
        self.qcount[queue] += 1
        dom = f"{queue}{lane}"
        op = _Op(QUEUES[queue], dom, fn, tag)
        self._dep(op, self.last_in_domain[dom], "sraw")
        return self._record(op, reads, writes, ())

    def barrier(self, tag="barrier"):
        lasts = [o for o in self.last_in_domain.values() if o is not None]
        for e in ENGINES:
            o = _Op(e, e, lambda eng: eng.nop(), tag)
            for p in lasts:
                self._dep(o, p, "sraw")
            self._record(o, (), (), ())

    def finish(self, eng="gpsimd", tag="finish"):
        lasts = [o for o in self.last_in_domain.values() if o is not None]
        o = _Op(eng, eng, lambda e: e.nop(), tag)
        for p in lasts:
            self._dep(o, p, "sraw")
        self._record(o, (), (), ())

    def emit(self, ctx):
        nc = self.nc
        for p in self.ops:
            for d in p.deps:
                d.inc_needed = True
        counters = {d: 0 for d in DOMAINS}
        for o in self.ops:
            if o.domain not in ENGINES:
                o.inc_needed = True
            if o.inc_needed:
                counters[o.domain] += 1 if o.domain in ENGINES else 16
            o.inc_val = counters[o.domain]
        lastclock = {e: {d: 0 for d in DOMAINS} for e in ENGINES}
        own_waited = {e: 0 for e in ENGINES}
        for o in self.ops:
            clk = dict(lastclock[o.eng])
            waits = []
            best = {}
            for p in o.deps:
                q = best.get(p.domain)
                if q is None or p.inc_val > q.inc_val:
                    best[p.domain] = p
            for p in sorted(best.values(), key=lambda p: (p.domain, p.inc_val)):
                if p.domain == o.eng:
                    if own_waited[o.eng] < p.inc_val:
                        waits.append((p.domain, p.inc_val))
                        own_waited[o.eng] = p.inc_val
                    if clk[p.domain] < p.inc_val:
                        clk[p.domain] = p.inc_val
                elif clk[p.domain] < p.inc_val:
                    waits.append((p.domain, p.inc_val))
                    clk[p.domain] = p.inc_val
                    for d, v in p.clock.items():
                        if clk[d] < v:
                            clk[d] = v
            if o.inc_needed and o.domain in ENGINES and clk[o.domain] < o.inc_val:
                clk[o.domain] = o.inc_val
            o.waits = waits
            o.clock = clk
            lastclock[o.eng] = clk
        for d in DOMAINS:
            if counters[d] > 0:
                self.sems[d] = ctx.enter_context(nc.semaphore(f"sem_{d}"))
        block = ctx.enter_context(nc.Block())
        builder = self

        def run_engine(engine_obj, ename):
            for o in builder.by_eng[ename]:
                for d, v in o.waits:
                    engine_obj.wait_ge(builder.sems[d], v)
                inst = o.fn(engine_obj)
                if o.inc_needed:
                    inst.then_inc(builder.sems[o.domain],
                                  1 if o.domain in ENGINES else 16)

        for ename in ENGINES:
            if self.by_eng[ename]:
                getattr(block, ename)(lambda e, _n=ename: run_engine(e, _n))

    def stats(self):
        return {"ops": len(self.ops),
                "waits": sum(len(o.waits) for o in self.ops),
                "per_engine": {e: len(v) for e, v in self.by_eng.items()}}


class _View:
    __slots__ = ("mem", "name", "off", "shape", "dtype")

    def __init__(self, mem, name, off, shape, dtype):
        self.mem = mem
        self.name = name
        self.off = off
        self.shape = shape
        self.dtype = dtype

    def ap(self):
        esz = mybir.dt.size(self.dtype)
        elems = math.prod(self.shape[1:])
        base = self.mem.handle.ap().bitcast(self.dtype)
        v = base[0:self.shape[0], self.off // esz: self.off // esz + elems]
        if len(self.shape) > 2:
            dims = " ".join(f"d{i}" for i in range(1, len(self.shape)))
            kw = {f"d{i}": s for i, s in enumerate(self.shape[1:], 1)}
            v = v.rearrange(f"p ({dims}) -> p {dims}", **kw)
        return v

    def k(self, *sub):
        return (self.name,) + sub


class Mem:
    def __init__(self, nc, cap_bytes_per_partition):
        self.nc = nc
        self.cap = cap_bytes_per_partition
        self.off = 0
        self.peak = 0
        self.handle = None

    def alloc(self, name, shape, dtype, align=64):
        nbytes = math.prod(shape[1:]) * mybir.dt.size(dtype)
        self.off = (self.off + align - 1) // align * align
        off = self.off
        self.off += nbytes
        self.peak = max(self.peak, self.off)
        assert self.off <= self.cap, \
            f"SBUF arena overflow at {name}: {self.off} > {self.cap}"
        return _View(self, name, off, shape, dtype)

    def alias(self, view, shape, dtype):
        nbytes = math.prod(shape[1:]) * mybir.dt.size(dtype)
        avail = math.prod(view.shape[1:]) * mybir.dt.size(view.dtype)
        assert nbytes <= avail, f"alias too big on {view.name}"
        return _View(self, view.name, view.off, shape, dtype)

    def mark(self):
        return self.off

    def release(self, m):
        self.off = m

    def materialize(self, ctx):
        self.handle = ctx.enter_context(
            self.nc.sbuf_tensor("arena", [128, self.cap], mybir.dt.uint8))


INPUT_NAMES = [
    "L0_attn_a_x", "L0_attn_a_ln_g", "L0_attn_a_ln_b", "L0_attn_a_W_in",
    "L0_attn_a_conv_w", "L0_attn_a_conv_b", "L0_attn_a_W_xp", "L0_attn_a_W_dt",
    "L0_attn_a_b_dt", "L0_attn_a_A_log", "L0_attn_a_D", "L0_attn_a_W_out",
    "L0_attn_b_W_q", "L0_attn_b_W_k", "L0_attn_b_W_v", "L0_attn_b_W_o",
    "L0_ffn_rw1", "L0_ffn_rb1", "L0_ffn_rw2", "L0_ffn_rb2",
    "L0_ffn_we1", "L0_ffn_be1", "L0_ffn_we2", "L0_ffn_be2",
]

SHAPES = {
    "L0_attn_a_x": [T, DM], "L0_attn_a_ln_g": [DM], "L0_attn_a_ln_b": [DM],
    "L0_attn_a_W_in": [2 * DI, DM], "L0_attn_a_conv_w": [DI, 4],
    "L0_attn_a_conv_b": [DI], "L0_attn_a_W_xp": [96, DI],
    "L0_attn_a_W_dt": [DI, DTR], "L0_attn_a_b_dt": [DI],
    "L0_attn_a_A_log": [DI, DST], "L0_attn_a_D": [DI],
    "L0_attn_a_W_out": [DM, DI],
    "L0_attn_b_W_q": [DM, DM], "L0_attn_b_W_k": [DM, DM],
    "L0_attn_b_W_v": [DM, DM], "L0_attn_b_W_o": [DM, DM],
    "L0_ffn_rw1": [DM, DM], "L0_ffn_rb1": [DM], "L0_ffn_rw2": [DM, NE],
    "L0_ffn_rb2": [NE], "L0_ffn_we1": [NE, DM, HF], "L0_ffn_be1": [NE, HF],
    "L0_ffn_we2": [NE, HF, DM], "L0_ffn_be2": [NE, DM],
}


def _rope_tables():
    inv = 1.0 / (10000.0 ** (np.arange(0, DKH, 2, dtype=np.float64) / DKH))
    phase = np.arange(T, dtype=np.float64)[:, None] * inv[None, :]
    cos = np.tile(np.cos(phase), (1, NH)).astype(np.float32)
    sin = np.tile(np.sin(phase), (1, NH)).astype(np.float32)
    return cos, sin


def build_program(phases=None):
    phases = _PHASES if phases is None else phases
    nc = bass.Bass("TRN2", target_bir_lowering=False, debug=False)

    di = {n: nc.dram_tensor(n, SHAPES[n], F32, kind="ExternalInput").ap()
          for n in INPUT_NAMES}
    out = nc.dram_tensor("out", [T, DM], F32, kind="ExternalOutput").ap()

    x_state = nc.dram_tensor("x_state", [T, DM], F32, kind="Internal").ap()
    z_scr = nc.dram_tensor("z_scr", [NCH, 16, 128, CH], F32, kind="Internal").ap()
    uc_scr = nc.dram_tensor("uc_scr", [NCH, 16, 128, CH], F32, kind="Internal").ap()
    winT_scr = nc.dram_tensor("winT_scr", [8, 128, 2 * DI], BF16, kind="Internal").ap()
    woutT_scr = nc.dram_tensor("woutT_scr", [16, 128, DM], BF16, kind="Internal").ap()
    wT_scr = {w: nc.dram_tensor(f"w{w}T_scr", [8, 128, DM], BF16,
                                kind="Internal").ap()
              for w in ("q", "k", "v", "o")}
    we1_bf = nc.dram_tensor("we1_bf", [NE, 4, 128, 8192], BF16, kind="Internal").ap()
    we2_bf = nc.dram_tensor("we2_bf", [NE, 4, 128, 8192], BF16, kind="Internal").ap()
    q_scr = nc.dram_tensor("q_scr", [NCH, 8, 128, CH], BF16, kind="Internal").ap()
    k_scr = nc.dram_tensor("k_scr", [NCH, 8, 128, CH], BF16, kind="Internal").ap()

    idt_bf_t = nc.inline_tensor(np.eye(128, dtype=ml_dtypes.bfloat16), "c_idtbf").ap()
    idt_f_t = nc.inline_tensor(np.eye(128, dtype=np.float32), "c_idtf").ap()
    oneh_np = np.zeros((16, 16 * 128), dtype=np.float32)
    for n in range(16):
        oneh_np[n, n * 128:(n + 1) * 128] = 1.0
    oneh_t = nc.inline_tensor(oneh_np, "c_oneh").ap()
    ones1_t = nc.inline_tensor(np.ones((1, 128), dtype=np.float32), "c_ones1").ap()
    cos_t = nc.dram_tensor("c_cos", [T, 512], F32, kind="ExternalInput").ap()
    sin_t = nc.dram_tensor("c_sin", [T, 512], F32, kind="ExternalInput").ap()

    b = Builder(nc)
    mem = Mem(nc, 206 * 1024)

    P = {}
    P["idtb"] = mem.alloc("idtb", [128, 128], BF16)
    P["idtf"] = mem.alloc("idtf", [128, 128], F32)
    P["oneh"] = mem.alloc("oneh", [16, 16 * 128], F32)
    P["ones1"] = mem.alloc("ones1", [1, 128], F32)
    P["lng"] = mem.alloc("lng", [128, 8], F32)
    P["lnb"] = mem.alloc("lnb", [128, 8], F32)
    P["convw"] = mem.alloc("convw", [128, 16, 4], F32)
    P["convb"] = mem.alloc("convb", [128, 16], F32)
    P["bdt"] = mem.alloc("bdt", [128, 16], F32)
    P["Dv"] = mem.alloc("Dv", [128, 16], F32)
    P["A"] = mem.alloc("A", [128, 16, 16], F32)
    P["wxpT"] = mem.alloc("wxpT", [128, 16, 96], BF16)
    P["wdtT"] = mem.alloc("wdtT", [64, 16, 128], BF16)
    P["rb1T"] = mem.alloc("rb1T", [128, 8], F32)
    P["rb2bc"] = mem.alloc("rb2bc", [128, 8], F32)
    P["be1T"] = mem.alloc("be1T", [128, 256], F32)
    P["be2n"] = mem.alloc("be2n", [8, 1024], F32)
    P["rw2"] = mem.alloc("rw2", [128, 8, 8], BF16)
    P["hst"] = mem.alloc("hst", [128, 256], F32)
    P["cst"] = mem.alloc("cst", [128, 4], F32)

    with ExitStack() as ctx:
        mem.materialize(ctx)
        psA = ctx.enter_context(nc.psum_tensor("psA", [128, 2048], F32))
        psB = ctx.enter_context(nc.psum_tensor("psB", [128, 1024], F32))
        psT = ctx.enter_context(nc.psum_tensor("psT", [128, 2048], BF16))

        def pA(i):
            return psA[:, i * 512:(i + 1) * 512]

        def pB(i):
            return psB[:, i * 512:(i + 1) * 512]

        def pT(i):
            return psT[:, (i % 2) * 1024:(i % 2) * 1024 + 128]

        def pTk(i):
            return ("psT", i % 2)

        def trbf(src_ap, src_keys, dst_ap, dst_keys, slot, tag=""):
            b.op("tensor", lambda e, s=src_ap: e.transpose(
                pT(slot), s, P["idtb"].ap()),
                reads=list(src_keys) + [P["idtb"].k()], writes=[pTk(slot)],
                tag="tr" + tag)
            b.op("vector", lambda e, d=dst_ap: e.tensor_copy(d, pT(slot)),
                 reads=[pTk(slot)], writes=list(dst_keys), tag="te" + tag)

        def ln_stats(xap, xkeys, stv, i, sdump, tag):
            negmu = stv.ap()[:, i, 0:1]
            rstd = stv.ap()[:, i, 1:2]
            s_ = stv.ap()[:, i, 2:3]
            q_ = stv.ap()[:, i, 3:4]
            kneg, krs, ks, kq = stv.k(i, 0), stv.k(i, 1), stv.k(i, 2), stv.k(i, 3)
            b.op("vector", lambda e: e.reduce_sum(s_, xap, AX.X),
                 reads=list(xkeys), writes=[ks], tag="lns" + tag)
            b.op("scalar", lambda e: e.activation(
                sdump.ap(), xap, AF.Square, accum_out=q_),
                reads=list(xkeys), writes=[kq, sdump.k()], tag="lnq" + tag)
            b.op("vector", lambda e: e.tensor_scalar_mul(negmu, s_, -1.0 / DM),
                 sreads=[ks], writes=[kneg], tag="lnm" + tag)
            b.op("vector", lambda e: e.tensor_scalar_mul(rstd, q_, 1.0 / DM),
                 sreads=[kq], writes=[krs], tag="lnv" + tag)
            b.op("vector", lambda e: e.tensor_mul(s_, negmu, negmu),
                 reads=[kneg], writes=[ks], tag="lnv2" + tag)
            b.op("vector", lambda e: e.tensor_sub(rstd, rstd, s_),
                 reads=[krs, ks], writes=[krs], tag="lnv3" + tag)
            b.op("scalar", lambda e: e.activation(
                rstd, rstd, AF.Ln, bias=P["cst"].ap()[:, 0:1]),
                reads=[krs], sreads=[P["cst"].k()], writes=[krs],
                tag="lnv4" + tag)
            b.op("scalar", lambda e: e.activation(
                rstd, rstd, AF.Exp, scale=P["cst"].ap()[:, 1:2]),
                reads=[krs], sreads=[P["cst"].k()], writes=[krs],
                tag="lnv5" + tag)
            return negmu, rstd, kneg, krs

        def setup():
            m0 = mem.mark()
            b.op("vector", lambda e: e.memset(P["cst"].ap()[:, 0:1],
                                              float(LNEPS)),
                 writes=[P["cst"].k()], tag="c0")
            b.op("vector", lambda e: e.memset(P["cst"].ap()[:, 1:2], -0.5),
                 reads=[P["cst"].k()], writes=[P["cst"].k()], tag="c1")
            b.op("vector", lambda e: e.memset(P["cst"].ap()[:, 2:3], 0.125),
                 reads=[P["cst"].k()], writes=[P["cst"].k()], tag="c2")
            b.op("vector", lambda e: e.memset(P["cst"].ap()[:, 3:4], 1.0),
                 reads=[P["cst"].k()], writes=[P["cst"].k()], tag="c3")
            for nm, src in [("idtb", idt_bf_t), ("idtf", idt_f_t),
                            ("oneh", oneh_t), ("ones1", ones1_t)]:
                b.dma("qsync", lambda e, d=P[nm].ap(), s=src: e.dma_start(
                    out=d, in_=s), writes=[P[nm].k()], tag="ld_" + nm)

            for piece in range(4):
                def ldc(e, piece=piece):
                    return e.dma_start(
                        out=P["convw"].ap()[:, piece * 4:(piece + 1) * 4, :],
                        in_=di["L0_attn_a_conv_w"][piece * 512:(piece + 1) * 512, :]
                        .rearrange("(t p) k -> p t k", p=128))
                b.dma("qsync", ldc, writes=[P["convw"].k()], tag="ld_convw")

            atmp = mem.alloc("atmp", [128, 16, 16], F32)
            for piece in range(4):
                def lda(e, piece=piece):
                    return e.dma_start(
                        out=atmp.ap()[:, piece * 4:(piece + 1) * 4, :],
                        in_=di["L0_attn_a_A_log"][piece * 512:(piece + 1) * 512, :]
                        .rearrange("(t p) n -> p t n", p=128))
                b.dma("qsync", lda, writes=[atmp.k()], tag="ld_alog")
            b.op("scalar", lambda e: e.activation(P["A"].ap(), atmp.ap(),
                                                  AF.Exp,
                                                  scale=P["cst"].ap()[:, 3:4]),
                 reads=[atmp.k()], sreads=[P["cst"].k()],
                 writes=[P["A"].k()], tag="expA")
            b.op("vector", lambda e: e.tensor_scalar_mul(
                P["A"].ap(), P["A"].ap(), -1.0),
                reads=[P["A"].k()], writes=[P["A"].k()], tag="negA")

            vtmp = mem.alloc("vtmp", [128, 128], F32)

            def vec_cols(src, n, dst, dtag):
                b.dma("qsync", lambda e: e.dma_start(
                    out=vtmp.ap()[0:n, :],
                    in_=src.rearrange("(r c) -> r c", c=128)),
                    writes=[vtmp.k()], tag="ldv" + dtag)
                b.op("tensor", lambda e: e.transpose(
                    psB[:, 0:n], vtmp.ap()[0:n, :], P["idtf"].ap()[0:n, 0:n]),
                    reads=[vtmp.k(), P["idtf"].k()], writes=[("psB", 0)],
                    tag="trv" + dtag)
                b.op("vector", lambda e: e.tensor_copy(dst.ap()[:, 0:n],
                                                       psB[:, 0:n]),
                     reads=[("psB", 0)], writes=[dst.k()], tag="evv" + dtag)

            vec_cols(di["L0_attn_a_ln_g"], 8, P["lng"], "lng")
            vec_cols(di["L0_attn_a_ln_b"], 8, P["lnb"], "lnb")
            vec_cols(di["L0_attn_a_conv_b"], 16, P["convb"], "convb")
            vec_cols(di["L0_attn_a_b_dt"], 16, P["bdt"], "bdt")
            vec_cols(di["L0_attn_a_D"], 16, P["Dv"], "Dv")
            vec_cols(di["L0_ffn_rb1"], 8, P["rb1T"], "rb1")

            btmp = mem.alloc("btmp", [128, 2, 128], F32)
            b.dma("qsync", lambda e: e.dma_start(
                out=btmp.ap(),
                in_=di["L0_ffn_be1"].rearrange("e (ht p) -> (e ht) p", p=128)
                .rearrange("(u r) p -> r u p", u=2)),
                writes=[btmp.k()], tag="ld_be1")
            for u in range(2):
                b.op("tensor", lambda e, u=u: e.transpose(
                    pB(u)[:, 0:128], btmp.ap()[:, u, :], P["idtf"].ap()),
                    reads=[btmp.k(), P["idtf"].k()], writes=[("psB", u)],
                    tag="tr_be1")
                b.op("vector", lambda e, u=u: e.tensor_copy(
                    P["be1T"].ap()[:, u * 128:(u + 1) * 128], pB(u)[:, 0:128]),
                    reads=[("psB", u)], writes=[P["be1T"].k()], tag="ev_be1")
            b.dma("qsync", lambda e: e.dma_start(
                out=P["be2n"].ap(), in_=di["L0_ffn_be2"]),
                writes=[P["be2n"].k()], tag="ld_be2n")

            r2tmp = mem.alloc("r2tmp", [1, 8], F32)
            b.dma("qsync", lambda e: e.dma_start(
                out=r2tmp.ap(),
                in_=di["L0_ffn_rb2"].rearrange("(a e) -> a e", a=1)),
                writes=[r2tmp.k()], tag="ld_rb2")
            b.op("tensor", lambda e: e.matmul(
                psB[:, 0:8], P["ones1"].ap(), r2tmp.ap(), start=True, stop=True),
                reads=[r2tmp.k(), P["ones1"].k()], writes=[("psB", 0)],
                tag="mm_rb2")
            b.op("vector", lambda e: e.tensor_copy(P["rb2bc"].ap(), psB[:, 0:8]),
                 reads=[("psB", 0)], writes=[P["rb2bc"].k()], tag="ev_rb2")

            b.dma("qgp", lambda e: e.dma_start(
                out=P["rw2"].ap(),
                in_=di["L0_ffn_rw2"].rearrange("(k p) n -> p k n", p=128)),
                writes=[P["rw2"].k()], tag="ld_rw2")

            wxtmp = mem.alloc("wxtmp", [96, 2048], BF16)
            b.dma("qgp", lambda e: e.dma_start(
                out=wxtmp.ap(), in_=di["L0_attn_a_W_xp"]),
                writes=[wxtmp.k()], tag="ld_wxp")
            for t in range(16):
                def trx(e, t=t):
                    return e.transpose(pT(t)[:, 0:96],
                                       wxtmp.ap()[0:96, t * 128:(t + 1) * 128],
                                       P["idtb"].ap()[0:96, 0:96])
                b.op("tensor", trx, reads=[wxtmp.k(), P["idtb"].k()],
                     writes=[pTk(t)], tag="tr_wxp")
                b.op("vector", lambda e, t=t: e.tensor_copy(
                    P["wxpT"].ap()[:, t, :], pT(t)[:, 0:96]),
                    reads=[pTk(t)], writes=[P["wxpT"].k()], tag="ev_wxp")

            wdtmp = mem.alloc("wdtmp", [128, 16, 64], BF16)
            b.dma("qgp", lambda e: e.dma_start(
                out=wdtmp.ap(),
                in_=di["L0_attn_a_W_dt"].rearrange("(t p) r -> p t r", p=128)),
                writes=[wdtmp.k()], tag="ld_wdt")
            for t in range(16):
                def trd(e, t=t):
                    return e.transpose(pT(t)[0:64, :], wdtmp.ap()[:, t, :],
                                       P["idtb"].ap())
                b.op("tensor", trd, reads=[wdtmp.k(), P["idtb"].k()],
                     writes=[pTk(t)], tag="tr_wdt")
                b.op("vector", lambda e, t=t: e.tensor_copy(
                    P["wdtT"].ap()[0:64, t, :], pT(t)[0:64, :]),
                    reads=[pTk(t)], writes=[P["wdtT"].k()], tag="ev_wdt")

            wbuf = mem.alloc("wbuf", [128, 8, 2048], BF16)

            def w_to_scratch(src, rows, cols, scr, tag):
                ctiles = cols // 128
                stg = _View(mem, "stg" + tag, mem.off, [128, ctiles, rows], BF16)
                stg_end = stg.off + ctiles * rows * 2
                assert stg_end <= mem.cap
                mem.peak = max(mem.peak, stg_end)
                for pc in range(rows // 1024):
                    def ldw(e, pc=pc):
                        return e.dma_start(
                            out=wbuf.ap()[:, :, 0:cols],
                            in_=src[pc * 1024:(pc + 1) * 1024, :]
                            .rearrange("(et p) d -> p et d", p=128))
                    b.dma("qgp", ldw, writes=[wbuf.k()], tag="ldw" + tag)
                    for et in range(8):
                        for dt in range(ctiles):
                            slot = (et * ctiles + dt) % 8
                            trbf(wbuf.ap()[:, et, dt * 128:(dt + 1) * 128],
                                 [wbuf.k()],
                                 stg.ap()[:, dt, (pc * 8 + et) * 128:
                                          (pc * 8 + et) * 128 + 128],
                                 [stg.k(dt), ("stgreg",)], slot, tag)
                def stw(e):
                    return e.dma_start(out=scr.rearrange("dt p e -> p dt e"),
                                      in_=stg.ap())
                b.dma("qsync", stw, reads=[stg.k(dt) for dt in range(ctiles)],
                      writes=[("stgreg",)], tag="stw" + tag)

            msub = mem.mark()
            w_to_scratch(di["L0_attn_a_W_in"], 2 * DI, DM, winT_scr, "win")
            mem.release(msub)
            w_to_scratch(di["L0_attn_a_W_out"], DM, DI, woutT_scr, "wout")
            mem.release(msub)
            for w, nm in (("q", "L0_attn_b_W_q"), ("k", "L0_attn_b_W_k"),
                          ("v", "L0_attn_b_W_v"), ("o", "L0_attn_b_W_o")):
                w_to_scratch(di[nm], DM, DM, wT_scr[w], "w" + w)
                mem.release(msub)

            b.op("vector", lambda e: e.memset(P["hst"].ap(), 0.0),
                 writes=[P["hst"].k()] + [P["hst"].k(c) for c in range(256)],
                 tag="init_hst")
            mem.release(m0)

        def mamba():
            m0 = mem.mark()
            mx = mem.alloc("mx", [128, 4, 1024], F32)
            mxb = mem.alias(mx, [128, 4, 1024], BF16)
            my0 = mem.alloc("my0", [128, 4, 1024], F32)
            y2bf = mem.alias(my0, [128, 16, 512], BF16)
            st1 = mem.alloc("st1", [128, 4, 4], F32)
            st2 = mem.alloc("st2", [128, 4, 4], F32)
            sdump = mem.alloc("sdump", [128, 1024], F32)
            mxnt = mem.alloc("mxnt", [128, 8, 512], BF16)
            xastg = mem.alloc("xastg", [128, 1024], F32)
            wostg = mem.alloc("wostg", [128, 2, 512], F32)
            mu = mem.alloc("mu", [128, 16, 515], F32)
            muc16 = mem.alloc("mc16", [128, 16, 512], BF16)
            zbuf = mem.alias(muc16, [128, 8, 512], F32)
            ucstg = mem.alloc("ucstg", [128, 2, 512], F32)
            ucdg = mem.alloc("ucdg", [128, 4, 512], F32)
            mwin = mem.alloc("mwin", [128, 2, 8, 128], BF16)
            mwout = mem.alloc("mwout", [128, 1, 16, 256], BF16)
            mdtl = mem.alloc("mdtl", [64, 512], BF16)
            mbcs = mem.alloc("mbcs", [16, 2, 512], F32)
            mdtu = mem.alloc("mdtu", [128, 4, 512], F32)
            scB = mem.alloc("scB", [128, 2, 512], F32)
            scC = mem.alloc("scC", [128, 2, 512], F32)
            dA2 = mem.alloc("dA2", [128, 2, 512], F32)
            dBu2 = mem.alloc("dBu2", [128, 2, 512], F32)
            hs2 = mem.alloc("hs2", [128, 2, 512], F32)
            pr2 = mem.alloc("pr2", [128, 2, 512], BF16)
            spt = mem.alloc("spt", [128, 512], F32)
            spr = mem.alloc("spr", [128, 512], F32)
            ysb = mem.alloc("ysb", [128, 512], F32)
            zst = mem.alloc("zst", [128, 2, 512], F32)

            b.op("vector", lambda e: e.memset(mu.ap()[:, :, 0:3], 0.0),
                 writes=[("mu", "h", t) for t in range(16)], tag="init_halo")

            for c in range(MCHUNKS):
                def ldx(e, c=c):
                    return e.dma_start(
                        out=mx.ap(),
                        in_=di["L0_attn_a_x"][c * CH:(c + 1) * CH, :]
                        .rearrange("(mt p) d -> p mt d", p=128))
                b.dma("qsync", ldx, writes=[mx.k()], tag=f"m_ldx{c}")

                for mt in range(4):
                    negmu, rstd, kng, krs = ln_stats(
                        mx.ap()[:, mt, :], [mx.k()], st1, mt, sdump, f"a{c}{mt}")
                    def y0f(e, mt=mt, negmu=negmu, rstd=rstd):
                        return e.tensor_scalar(
                            my0.ap()[:, mt, :], mx.ap()[:, mt, :],
                            negmu, rstd, ALU.add, ALU.mult)
                    b.op("vector", y0f, reads=[mx.k()], sreads=[kng, krs],
                         writes=[my0.k()], tag=f"y0_{c}{mt}")
                    def xaf(e, mt=mt):
                        return e.tensor_add(xastg.ap(), mx.ap()[:, mt, :],
                                            my0.ap()[:, mt, :])
                    b.op("vector", xaf, reads=[mx.k(), my0.k()],
                         writes=[xastg.k()], tag=f"xa_{c}{mt}")
                    def stxa(e, c=c, mt=mt):
                        return e.dma_start(
                            out=x_state[c * CH + mt * 128: c * CH + (mt + 1) * 128, :],
                            in_=xastg.ap())
                    b.dma("qsync", stxa, reads=[xastg.k()],
                          writes=[("xst", c, mt)], tag=f"stxa{c}{mt}")
                    negmu2, rstd2, kng2, krs2 = ln_stats(
                        my0.ap()[:, mt, :], [my0.k()], st2, mt, sdump,
                        f"b{c}{mt}")
                    def t1f(e, mt=mt, negmu2=negmu2, rstd2=rstd2):
                        return e.tensor_scalar(
                            mxb.ap()[:, mt, :], my0.ap()[:, mt, :],
                            negmu2, rstd2, ALU.add, ALU.mult)
                    b.op("vector", t1f, reads=[my0.k()], sreads=[kng2, krs2],
                         writes=[mx.k()], tag=f"t1_{c}{mt}")

                for mt in range(4):
                    for dk in range(8):
                        slot = (mt * 8 + dk) % 8
                        b.op("tensor", lambda e, mt=mt, dk=dk, slot=slot:
                             e.transpose(pT(slot),
                                         mxb.ap()[:, mt, dk * 128:(dk + 1) * 128],
                                         P["idtb"].ap()),
                             reads=[mx.k(), P["idtb"].k()], writes=[pTk(slot)],
                             tag=f"trm{c}")
                        def gap(e, mt=mt, dk=dk, slot=slot):
                            return e.tensor_scalar(
                                mxnt.ap()[:, dk, mt * 128:(mt + 1) * 128],
                                pT(slot), P["lng"].ap()[:, dk:dk + 1],
                                P["lnb"].ap()[:, dk:dk + 1], ALU.mult, ALU.add)
                        b.op("vector", gap, reads=[pTk(slot)],
                             sreads=[P["lng"].k(), P["lnb"].k()],
                             writes=[mxnt.k(dk)], tag=f"gam{c}")

                if MSTAGE < 2:
                    continue
                for g in range(32):
                    par = g % 2
                    def ldwin(e, g=g, par=par):
                        return e.dma_start(
                            out=mwin.ap()[:, par],
                            in_=winT_scr[:, :, g * 128:(g + 1) * 128]
                            .rearrange("k p e -> p k e"))
                    b.dma("qsync", ldwin, writes=[mwin.k(par)], tag=f"lwin{c}{g}")
                    if True:
                        et = g
                        bank = et % 3
                        for kk in range(8):
                            def mmw(e, kk=kk, par=par, bank=bank):
                                return e.matmul(
                                    pA(bank),
                                    mwin.ap()[:, par, kk, :],
                                    mxnt.ap()[:, kk, :],
                                    start=(kk == 0), stop=(kk == 7))
                            b.op("tensor", mmw,
                                 reads=[mwin.k(par), mxnt.k(kk), ("psA", bank)],
                                 writes=[("psA", bank)], tag=f"mmwin{c}")
                        if et < 16:
                            def evu(e, et=et, bank=bank):
                                return e.copy(mu.ap()[:, et, 3:515], pA(bank))
                            b.op("scalar", evu, reads=[("psA", bank)],
                                 writes=[("mu", et)], tag=f"evu{c}")
                        else:
                            zt = et - 16
                            zp = zt % 2
                            def evz(e, bank=bank, zp=zp):
                                return e.activation(zst.ap()[:, zp, :], pA(bank),
                                                    AF.Silu,
                                                    scale=P["cst"].ap()[:, 3:4])
                            b.op("scalar", evz, reads=[("psA", bank)],
                                 sreads=[P["cst"].k()],
                                 writes=[zst.k(zp)], tag=f"evz{c}")
                            def stz(e, c=c, zt=zt, zp=zp):
                                return e.dma_start(out=z_scr[c, zt],
                                                   in_=zst.ap()[:, zp, :])
                            b.dma("qsca", stz, reads=[zst.k(zp)], tag=f"stz{c}")

                if MSTAGE < 3:
                    continue
                for t in range(16):
                    def cv0(e, t=t):
                        return e.tensor_scalar_mul(
                            spr.ap(), mu.ap()[:, t, 0:512],
                            P["convw"].ap()[:, t, 0:1])
                    b.op("vector", cv0, reads=[("mu", t), ("mu", "h", t)],
                         sreads=[P["convw"].k()], writes=[spr.k()], tag=f"cv{c}")
                    for kk in range(1, 4):
                        def cvk(e, t=t, kk=kk):
                            return e.scalar_tensor_tensor(
                                spr.ap(), mu.ap()[:, t, kk:kk + 512],
                                P["convw"].ap()[:, t, kk:kk + 1],
                                spr.ap(), ALU.mult, ALU.add)
                        b.op("vector", cvk,
                             reads=[("mu", t), ("mu", "h", t), spr.k()],
                             sreads=[P["convw"].k()], writes=[spr.k()],
                             tag=f"cv{c}")
                    def hcp(e, t=t):
                        return e.tensor_copy(mu.ap()[:, t, 0:3],
                                             mu.ap()[:, t, 512:515])
                    b.op("vector", hcp, reads=[("mu", t)],
                         writes=[("mu", "h", t)], tag=f"hcp{c}")
                    up = t % 2
                    def sl32(e, t=t, up=up):
                        return e.activation(ucstg.ap()[:, up, :], spr.ap(),
                                            AF.Silu,
                                            bias=P["convb"].ap()[:, t:t + 1])
                    b.op("scalar", sl32, reads=[spr.k()],
                         sreads=[P["convb"].k()], writes=[ucstg.k(up)],
                         tag=f"sl{c}")
                    def stuc(e, c=c, t=t, up=up):
                        return e.dma_start(out=uc_scr[c, t],
                                           in_=ucstg.ap()[:, up, :])
                    b.dma("qsca", stuc, reads=[ucstg.k(up)],
                          writes=[("ucs", c, t)], tag=f"stuc{c}")
                    def slb(e, t=t, up=up):
                        return e.tensor_copy(muc16.ap()[:, t, :],
                                             ucstg.ap()[:, up, :])
                    b.op("vector", slb, reads=[ucstg.k(up)],
                         writes=[("mc16", t)], tag=f"slb{c}")

                if MSTAGE < 4:
                    continue
                for kk in range(16):
                    def mmx1(e, kk=kk):
                        return e.matmul(psB[0:64, 0:512],
                                        P["wxpT"].ap()[:, kk, 0:64],
                                        muc16.ap()[:, kk, :],
                                        start=(kk == 0), stop=(kk == 15))
                    b.op("tensor", mmx1,
                         reads=[P["wxpT"].k(), ("mc16", kk), ("psB", 0)],
                         writes=[("psB", 0)], tag=f"mmxa{c}")
                for kk in range(16):
                    def mmx2(e, kk=kk):
                        return e.matmul(psB[0:16, 512:1024],
                                        P["wxpT"].ap()[:, kk, 64:80],
                                        muc16.ap()[:, kk, :],
                                        start=(kk == 0), stop=(kk == 15))
                    b.op("tensor", mmx2,
                         reads=[P["wxpT"].k(), ("mc16", kk), ("psB", 1)],
                         writes=[("psB", 1)], tag=f"mmxb{c}")
                for kk in range(16):
                    def mmx3(e, kk=kk):
                        return e.matmul(pA(0)[0:16, :],
                                        P["wxpT"].ap()[:, kk, 80:96],
                                        muc16.ap()[:, kk, :],
                                        start=(kk == 0), stop=(kk == 15))
                    b.op("tensor", mmx3,
                         reads=[P["wxpT"].k(), ("mc16", kk), ("psA", 0)],
                         writes=[("psA", 0)], tag=f"mmxc{c}")
                b.op("scalar", lambda e: e.copy(mdtl.ap(), psB[0:64, 0:512]),
                     reads=[("psB", 0)], writes=[mdtl.k()], tag=f"edtl{c}")
                b.op("vector", lambda e: e.tensor_copy(
                    mbcs.ap()[:, 0, :], psB[0:16, 512:1024]),
                    reads=[("psB", 1)], writes=[mbcs.k("B")], tag=f"evB{c}")
                b.op("vector", lambda e: e.tensor_copy(
                    mbcs.ap()[:, 1, :], pA(0)[0:16, :]),
                    reads=[("psA", 0)], writes=[mbcs.k("C")], tag=f"evC{c}")

                if MSTAGE < 5:
                    continue
                for t in range(16):
                    bank = 1 + (t % 2)
                    def mmdt(e, t=t, bank=bank):
                        return e.matmul(pA(bank), P["wdtT"].ap()[:, t, :],
                                        mdtl.ap(), start=True, stop=True)
                    b.op("tensor", mmdt,
                         reads=[P["wdtT"].k(), mdtl.k(), ("psA", bank)],
                         writes=[("psA", bank)], tag=f"mmdt{c}")
                    if MSTAGE == 41:
                        def spc(e, bank=bank):
                            return e.tensor_copy(spr.ap(), pA(bank))
                        b.op("vector", spc, reads=[("psA", bank)],
                             writes=[spr.k()], tag=f"spc{c}")
                        continue
                    def sp1(e, t=t, bank=bank):
                        return e.tensor_scalar(
                            spr.ap(), pA(bank), P["bdt"].ap()[:, t:t + 1], 0.0,
                            ALU.add, ALU.max)
                    b.op("vector", sp1, reads=[("psA", bank), spr.k()],
                         sreads=[P["bdt"].k()], writes=[spr.k()], tag=f"sp1{c}")
                    if MSTAGE == 42:
                        continue
                    def sp2(e, t=t, bank=bank):
                        return e.tensor_scalar(
                            spt.ap(), pA(bank), P["bdt"].ap()[:, t:t + 1], 0.0,
                            ALU.add, ALU.min)
                    b.op("vector", sp2, reads=[("psA", bank), spt.k()],
                         sreads=[P["bdt"].k()], writes=[spt.k()], tag=f"sp2{c}")
                    def sp2b(e):
                        return e.tensor_sub(spt.ap(), spr.ap(), spt.ap())
                    b.op("vector", sp2b, reads=[spr.k(), spt.k()],
                         writes=[spt.k()], tag=f"sp2b{c}")
                    b.op("scalar", lambda e: e.activation(
                        spt.ap(), spt.ap(), AF.Exp, scale=-1.0),
                        reads=[spt.k()], writes=[spt.k()], tag=f"sp3{c}")
                    if MSTAGE == 43:
                        continue
                    b.op("scalar", lambda e: e.activation(
                        spt.ap(), spt.ap(), AF.Ln, bias=1.0),
                        reads=[spt.k()], writes=[spt.k()], tag=f"sp4{c}")
                    def sp5(e, t=t):
                        return e.tensor_add(mu.ap()[:, t, 3:515], spr.ap(),
                                            spt.ap())
                    b.op("vector", sp5, reads=[spr.k(), spt.k()],
                         writes=[("mu", t)], tag=f"sp5{c}")

                if MSTAGE < 6 or MSTAGE in (41, 42, 43):
                    continue
                def ldz(e, c=c):
                    return e.dma_start(
                        out=zbuf.ap(),
                        in_=z_scr[c, 0:8].rearrange("zt p t -> p zt t"))
                b.dma("qsync", ldz, writes=[("mc16", t) for t in range(16)],
                      tag=f"ldz{c}")

                it = 0
                for dg in range(4):
                    def lduc(e, c=c, dg=dg):
                        return e.dma_start(
                            out=ucdg.ap(),
                            in_=uc_scr[c, dg * 4:(dg + 1) * 4]
                            .rearrange("t p x -> p t x"))
                    b.dma("qsync", lduc,
                          reads=[("ucs", c, dg * 4 + j) for j in range(4)],
                          writes=[ucdg.k()], tag=f"lduc{c}{dg}")
                    for dj in range(4):
                        t = dg * 4 + dj
                        def dtu(e, t=t, dj=dj):
                            return e.tensor_mul(mdtu.ap()[:, dj, :],
                                                mu.ap()[:, t, 3:515],
                                                ucdg.ap()[:, dj, :])
                        b.op("vector", dtu, reads=[("mu", t), ucdg.k()],
                             writes=[mdtu.k(dj)], tag=f"dtu{c}")
                    for n in range(16):
                        pn = n % 2
                        def bcB(e, n=n):
                            return e.matmul(pB(0),
                                            P["oneh"].ap()[:, n * 128:(n + 1) * 128],
                                            mbcs.ap()[:, 0, :],
                                            start=True, stop=True)
                        b.op("tensor", bcB,
                             reads=[P["oneh"].k(), mbcs.k("B"), ("psB", 0)],
                             writes=[("psB", 0)], tag=f"bcB{c}")
                        def bcC(e, n=n):
                            return e.matmul(pB(1),
                                            P["oneh"].ap()[:, n * 128:(n + 1) * 128],
                                            mbcs.ap()[:, 1, :],
                                            start=True, stop=True)
                        b.op("tensor", bcC,
                             reads=[P["oneh"].k(), mbcs.k("C"), ("psB", 1)],
                             writes=[("psB", 1)], tag=f"bcC{c}")
                        b.op("scalar", lambda e, pn=pn: e.copy(
                            scB.ap()[:, pn, :], pB(0)),
                            reads=[("psB", 0)], writes=[scB.k(pn)], tag=f"cB{c}")
                        b.op("scalar", lambda e, pn=pn: e.copy(
                            scC.ap()[:, pn, :], pB(1)),
                            reads=[("psB", 1)], writes=[scC.k(pn)], tag=f"cC{c}")
                        for dj in range(4):
                            t = dg * 4 + dj
                            col = t * 16 + n
                            pi = it % 2
                            it += 1
                            def fdA(e, t=t, n=n, pi=pi):
                                return e.activation(
                                    dA2.ap()[:, pi, :], mu.ap()[:, t, 3:515],
                                    AF.Exp, scale=P["A"].ap()[:, t, n:n + 1])
                            b.op("scalar", fdA, reads=[("mu", t)],
                                 sreads=[P["A"].k()], writes=[dA2.k(pi)],
                                 tag=f"dA{c}")
                            def fdBu(e, dj=dj, pn=pn, pi=pi):
                                return e.tensor_mul(dBu2.ap()[:, pi, :],
                                                    mdtu.ap()[:, dj, :],
                                                    scB.ap()[:, pn, :])
                            b.op("gpsimd", fdBu,
                                 reads=[mdtu.k(dj), scB.k(pn)],
                                 writes=[dBu2.k(pi)], tag=f"dBu{c}")
                            def fscan(e, col=col, pi=pi):
                                return e.tensor_tensor_scan(
                                    hs2.ap()[:, pi, :], dA2.ap()[:, pi, :],
                                    dBu2.ap()[:, pi, :],
                                    P["hst"].ap()[:, col:col + 1],
                                    ALU.mult, ALU.add)
                            b.op("vector", fscan,
                                 reads=[dA2.k(pi), dBu2.k(pi)],
                                 sreads=[P["hst"].k(col)],
                                 writes=[hs2.k(pi)], tag=f"scan{c}")
                            def fhs(e, col=col, pi=pi):
                                return e.tensor_copy(
                                    P["hst"].ap()[:, col:col + 1],
                                    hs2.ap()[:, pi, 511:512])
                            b.op("vector", fhs, reads=[hs2.k(pi)],
                                 writes=[P["hst"].k(col)], tag=f"hs{c}")
                            def fpr(e, pn=pn, pi=pi):
                                return e.tensor_mul(pr2.ap()[:, pi, :],
                                                    hs2.ap()[:, pi, :],
                                                    scC.ap()[:, pn, :])
                            b.op("vector", fpr,
                                 reads=[hs2.k(pi), scC.k(pn)],
                                 writes=[pr2.k(pi)], tag=f"pr{c}")
                            def fyac(e, dj=dj, n=n, pi=pi):
                                return e.matmul(pA(dj), P["idtb"].ap(),
                                                pr2.ap()[:, pi, :],
                                                start=(n == 0), stop=(n == 15))
                            b.op("tensor", fyac,
                                 reads=[pr2.k(pi), P["idtb"].k(), ("psA", dj)],
                                 writes=[("psA", dj)], tag=f"yac{c}")
                    for dj in range(4):
                        t = dg * 4 + dj
                        def yev(e, t=t, dj=dj):
                            return e.scalar_tensor_tensor(
                                ysb.ap(), ucdg.ap()[:, dj, :],
                                P["Dv"].ap()[:, t:t + 1], pA(dj),
                                ALU.mult, ALU.add)
                        b.op("vector", yev,
                             reads=[ucdg.k(), ("psA", dj), ysb.k()],
                             sreads=[P["Dv"].k()], writes=[ysb.k()],
                             tag=f"yev{c}")
                        def y2(e, t=t):
                            return e.tensor_mul(y2bf.ap()[:, t, :], ysb.ap(),
                                                zbuf.ap()[:, t % 8, :])
                        b.op("vector", y2,
                             reads=[ysb.k(), ("mc16", 0), my0.k()],
                             writes=[my0.k()], tag=f"y2{c}")
                    if dg == 1:
                        def ldz2(e, c=c):
                            return e.dma_start(
                                out=zbuf.ap(),
                                in_=z_scr[c, 8:16].rearrange("zt p t -> p zt t"))
                        b.dma("qsync", ldz2,
                              writes=[("mc16", t) for t in range(16)],
                              tag=f"ldz2{c}")

                if MSTAGE < 7 or MSTAGE in (41, 42, 43):
                    continue
                for piece in range(4):
                    wp = 0
                    def ldwo(e, piece=piece, wp=wp):
                        return e.dma_start(
                            out=mwout.ap()[:, wp],
                            in_=woutT_scr[:, :, piece * 256:(piece + 1) * 256]
                            .rearrange("k p d -> p k d"))
                    b.dma("qsync", ldwo, writes=[("mwout", wp)],
                          tag=f"lwo{c}{piece}")
                    for mt in range(4):
                        bank = mt % 2
                        for kk in range(16):
                            def mmo(e, mt=mt, kk=kk, wp=wp, bank=bank):
                                return e.matmul(
                                    pB(bank)[:, 0:256],
                                    y2bf.ap()[:, kk, mt * 128:(mt + 1) * 128],
                                    mwout.ap()[:, wp, kk, :],
                                    start=(kk == 0), stop=(kk == 15))
                            b.op("tensor", mmo,
                                 reads=[my0.k(), ("mwout", wp), ("psB", bank)],
                                 writes=[("psB", bank)], tag=f"mmo{c}")
                        sp = mt % 2
                        def oev(e, bank=bank, sp=sp):
                            return e.tensor_copy(wostg.ap()[:, sp, 0:256],
                                                 pB(bank)[:, 0:256])
                        b.op("vector", oev, reads=[("psB", bank)],
                             writes=[wostg.k(sp)], tag=f"oev{c}")
                        def oacc(e, c=c, mt=mt, piece=piece, sp=sp):
                            return e.dma_start(
                                out=x_state[c * CH + mt * 128:
                                            c * CH + (mt + 1) * 128,
                                            piece * 256:(piece + 1) * 256],
                                in_=wostg.ap()[:, sp, 0:256],
                                accum_op=ALU.add)
                        b.dma("qgp", oacc, reads=[wostg.k(sp)],
                              writes=[("xst", c, mt)], tag=f"oacc{c}")
            mem.release(m0)

        def ffn(call):
            m0 = mem.mark()
            rw1 = mem.alloc("rw1", [128, 8, 1024], BF16)
            b.dma("qgp", lambda e: e.dma_start(
                out=rw1.ap(),
                in_=di["L0_ffn_rw1"].rearrange("(k p) o -> p k o", p=128)),
                writes=[rw1.k()], tag=f"ld_rw1_{call}")
            fx = mem.alloc("fx", [128, 8, 1024], F32)
            g1t = mem.alias(fx, [128, 8, 1024], BF16)
            fxacc = mem.alloc("fxacc", [128, 8, 1024], F32)
            fxnb = mem.alloc("fxnb", [128, 1024], BF16)
            fxnt = mem.alloc("fxnt", [128, 8, 1024], BF16)
            we1b = mem.alloc("we1b", [128, 2, 8, 1024], BF16)
            we2b = mem.alloc("we2b", [128, 2, 8, 1024], BF16)
            hb = mem.alloc("hb", [128, 8, 512], BF16)
            xn32 = mem.alloc("xn32", [128, 1024], F32)
            wnat = mem.alloc("wnat", [128, 8, 8], F32)
            fwT = mem.alloc("fwT", [8, 1024], F32)
            lgv = mem.alloc("lgv", [128, 8], F32)
            tk1 = mem.alloc("tk1", [128, 8], F32)
            tk2 = mem.alloc("tk2", [128, 4], F32)
            stf = mem.alloc("stf", [128, 8, 4], F32)
            sdump = mem.alias(xn32, [128, 1024], F32)

            first_conv = (call == 0)
            for q in range(4):
                q0 = q * 1024
                def ldx(e, q0=q0):
                    return e.dma_start(
                        out=fx.ap(),
                        in_=x_state[q0:q0 + 1024, :]
                        .rearrange("(mt p) d -> p mt d", p=128))
                b.dma("qsync", ldx,
                      reads=[("xst", (q0 // CH) + i, mt) for i in range(2)
                             for mt in range(4)],
                      writes=[fx.k()], tag=f"f{call}ldx{q}")

                for mt in range(8):
                    negmu, rstd, kng, krs = ln_stats(
                        fx.ap()[:, mt, :], [fx.k()], stf, mt, sdump,
                        f"f{call}{q}{mt}")
                    def xnf(e, mt=mt, negmu=negmu, rstd=rstd):
                        return e.tensor_scalar(
                            xn32.ap(), fx.ap()[:, mt, :], negmu, rstd,
                            ALU.add, ALU.mult)
                    b.op("vector", xnf, reads=[fx.k()], sreads=[kng, krs],
                         writes=[xn32.k()], tag=f"fxn{call}{q}{mt}")
                    b.op("vector", lambda e, mt=mt: e.tensor_add(
                        fxacc.ap()[:, mt, :], fx.ap()[:, mt, :], xn32.ap()),
                        reads=[fx.k(), xn32.k()], writes=[fxacc.k(mt)],
                        tag=f"fxa{call}{q}{mt}")
                    b.op("vector", lambda e, mt=mt: e.tensor_copy(
                        fxnb.ap(), xn32.ap()),
                        reads=[xn32.k()], writes=[fxnb.k()],
                        tag=f"fnb{call}{q}{mt}")
                    for dk in range(8):
                        slot = (mt * 8 + dk) % 8
                        b.op("tensor", lambda e, dk=dk, slot=slot:
                             e.transpose(pT(slot),
                                         fxnb.ap()[:, dk * 128:(dk + 1) * 128],
                                         P["idtb"].ap()),
                             reads=[fxnb.k(), P["idtb"].k()],
                             writes=[pTk(slot)], tag=f"ftr{call}")
                        b.op("vector", lambda e, mt=mt, dk=dk, slot=slot:
                             e.tensor_copy(
                                 fxnt.ap()[:, dk, mt * 128:(mt + 1) * 128],
                                 pT(slot)),
                             reads=[pTk(slot)], writes=[fxnt.k(dk)],
                             tag=f"fte{call}")

                for ot in range(8):
                    for th in range(2):
                        bank = (ot * 2 + th) % 3
                        for kk in range(8):
                            def mg(e, ot=ot, th=th, kk=kk, bank=bank):
                                return e.matmul(
                                    pA(bank),
                                    rw1.ap()[:, kk, ot * 128:(ot + 1) * 128],
                                    fxnt.ap()[:, kk, th * 512:(th + 1) * 512],
                                    start=(kk == 0), stop=(kk == 7))
                            b.op("tensor", mg,
                                 reads=[rw1.k(), fxnt.k(kk), ("psA", bank)],
                                 writes=[("psA", bank)], tag=f"fg1{call}")
                        def eg(e, ot=ot, th=th, bank=bank):
                            return e.activation(
                                g1t.ap()[:, ot, th * 512:(th + 1) * 512],
                                pA(bank), AF.Silu,
                                bias=P["rb1T"].ap()[:, ot:ot + 1])
                        b.op("scalar", eg, reads=[("psA", bank)],
                             sreads=[P["rb1T"].k()], writes=[fx.k()],
                             tag=f"fge{call}")
                for mt in range(8):
                    for kk in range(8):
                        def ml(e, mt=mt, kk=kk):
                            return e.matmul(
                                psB[:, 0:8],
                                g1t.ap()[:, kk, mt * 128:(mt + 1) * 128],
                                P["rw2"].ap()[:, kk, :],
                                start=(kk == 0), stop=(kk == 7))
                        b.op("tensor", ml,
                             reads=[fx.k(), P["rw2"].k(), ("psB", 0)],
                             writes=[("psB", 0)], tag=f"flg{call}")
                    b.op("vector", lambda e: e.tensor_add(
                        lgv.ap(), psB[:, 0:8], P["rb2bc"].ap()),
                        reads=[("psB", 0), P["rb2bc"].k()], writes=[lgv.k()],
                        tag=f"flgv{call}")
                    b.op("vector", lambda e: e.reduce_max(
                        tk2.ap()[:, 0:1], lgv.ap(), AX.X),
                        reads=[lgv.k()], writes=[("tk2", 0)], tag="t2a")
                    b.op("vector", lambda e: e.tensor_scalar(
                        tk1.ap(), lgv.ap(), tk2.ap()[:, 0:1], None,
                        ALU.is_equal),
                        reads=[lgv.k()], sreads=[("tk2", 0)], writes=[tk1.k()],
                        tag="t2b")
                    b.op("vector", lambda e: e.tensor_scalar_mul(
                        tk1.ap(), tk1.ap(), -1e30),
                        reads=[tk1.k()], writes=[tk1.k()], tag="t2c")
                    b.op("vector", lambda e: e.tensor_add(
                        tk1.ap(), tk1.ap(), lgv.ap()),
                        reads=[tk1.k(), lgv.k()], writes=[tk1.k()], tag="t2d")
                    b.op("vector", lambda e: e.reduce_max(
                        tk2.ap()[:, 1:2], tk1.ap(), AX.X),
                        reads=[tk1.k()], writes=[("tk2", 1)], tag="t2e")
                    b.op("vector", lambda e: e.tensor_scalar(
                        tk1.ap(), lgv.ap(), tk2.ap()[:, 1:2], None, ALU.is_ge),
                        reads=[lgv.k()], sreads=[("tk2", 1)], writes=[tk1.k()],
                        tag="t2f")
                    b.op("vector", lambda e: e.reduce_max(
                        tk2.ap()[:, 2:3], lgv.ap(), AX.X, negate=True),
                        reads=[lgv.k()], writes=[("tk2", 2)], tag="t2g")
                    def t2h(e, mt=mt):
                        return e.activation(wnat.ap()[:, mt, :], lgv.ap(),
                                            AF.Exp, bias=tk2.ap()[:, 2:3])
                    b.op("scalar", t2h, reads=[lgv.k()], sreads=[("tk2", 2)],
                         writes=[wnat.k(mt)], tag="t2h")
                    def t2i(e, mt=mt):
                        return e.tensor_mul(wnat.ap()[:, mt, :],
                                            wnat.ap()[:, mt, :], tk1.ap())
                    b.op("vector", t2i, reads=[wnat.k(mt), tk1.k()],
                         writes=[wnat.k(mt)], tag="t2i")
                    def t2j(e, mt=mt):
                        return e.reduce_sum(tk2.ap()[:, 3:4],
                                            wnat.ap()[:, mt, :], AX.X)
                    b.op("vector", t2j, reads=[wnat.k(mt)], writes=[("tk2", 3)],
                         tag="t2j")
                    b.op("vector", lambda e: e.reciprocal(
                        tk2.ap()[:, 3:4], tk2.ap()[:, 3:4]),
                        reads=[("tk2", 3)], writes=[("tk2", 3)], tag="t2k")
                    def t2l(e, mt=mt):
                        return e.tensor_scalar(wnat.ap()[:, mt, :],
                                               wnat.ap()[:, mt, :],
                                               tk2.ap()[:, 3:4], None, ALU.mult)
                    b.op("vector", t2l, reads=[wnat.k(mt)], sreads=[("tk2", 3)],
                         writes=[wnat.k(mt)], tag="t2l")
                    def t2m(e, mt=mt):
                        return e.transpose(psB[0:8, 512:640],
                                           wnat.ap()[:, mt, :], P["idtf"].ap())
                    b.op("tensor", t2m, reads=[wnat.k(mt), P["idtf"].k(),
                                               ("psB", 1)],
                         writes=[("psB", 1)], tag="t2m")
                    def t2n(e, mt=mt):
                        return e.tensor_copy(
                            fwT.ap()[:, mt * 128:(mt + 1) * 128],
                            psB[0:8, 512:640])
                    b.op("vector", t2n, reads=[("psB", 1)], writes=[fwT.k()],
                         tag="t2n")

                for mt in range(8):
                    for nd in range(2):
                        def mbe(e, mt=mt, nd=nd):
                            return e.matmul(
                                pA(3)[:, 0:512],
                                fwT.ap()[:, mt * 128:(mt + 1) * 128],
                                P["be2n"].ap()[:, nd * 512:(nd + 1) * 512],
                                start=True, stop=True)
                        b.op("tensor", mbe,
                             reads=[fwT.k(), P["be2n"].k(), ("psA", 3)],
                             writes=[("psA", 3)], tag=f"fbe{call}")
                        def abe(e, mt=mt, nd=nd):
                            return e.tensor_add(
                                fxacc.ap()[:, mt, nd * 512:(nd + 1) * 512],
                                fxacc.ap()[:, mt, nd * 512:(nd + 1) * 512],
                                pA(3)[:, 0:512])
                        b.op("vector", abe, reads=[("psA", 3), fxacc.k(mt)],
                             writes=[fxacc.k(mt)], tag=f"fab{call}")

                for ee in range(NE):
                    for sl in range(4):
                        par = (ee * 4 + sl) % 2
                        if first_conv and q == 0:
                            def l1(e, ee=ee, sl=sl, par=par):
                                return e.dma_start(
                                    out=we1b.ap()[:, par],
                                    in_=di["L0_ffn_we1"][ee][:, sl * 1024:
                                                            (sl + 1) * 1024]
                                    .rearrange("(k p) h -> p k h", p=128))
                            b.dma("qgp", l1, writes=[("we1b", par)],
                                  tag=f"fl1{call}{q}")
                            def s1(e, ee=ee, sl=sl, par=par):
                                return e.dma_start(
                                    out=we1_bf[ee, sl],
                                    in_=we1b.ap()[:, par]
                                    .rearrange("p k h -> p (k h)"))
                            b.dma("qsca", s1, reads=[("we1b", par)],
                                  writes=[("w1s", ee, sl)], tag=f"fs1{call}")
                            def l2(e, ee=ee, sl=sl, par=par):
                                return e.dma_start(
                                    out=we2b.ap()[:, par],
                                    in_=di["L0_ffn_we2"][ee][sl * 1024:
                                                             (sl + 1) * 1024, :]
                                    .rearrange("(k p) d -> p k d", p=128))
                            b.dma("qgp", l2, writes=[("we2b", par)],
                                  tag=f"fl2{call}{q}")
                            def s2(e, ee=ee, sl=sl, par=par):
                                return e.dma_start(
                                    out=we2_bf[ee, sl],
                                    in_=we2b.ap()[:, par]
                                    .rearrange("p k h -> p (k h)"))
                            b.dma("qsca", s2, reads=[("we2b", par)],
                                  writes=[("w2s", ee, sl)], tag=f"fs2{call}")
                        else:
                            def l1(e, ee=ee, sl=sl, par=par):
                                return e.dma_start(
                                    out=we1b.ap()[:, par]
                                    .rearrange("p k h -> p (k h)"),
                                    in_=we1_bf[ee, sl])
                            b.dma("qgp", l1, reads=[("w1s", ee, sl)],
                                  writes=[("we1b", par)], tag=f"fl1{call}{q}")
                            def l2(e, ee=ee, sl=sl, par=par):
                                return e.dma_start(
                                    out=we2b.ap()[:, par]
                                    .rearrange("p k h -> p (k h)"),
                                    in_=we2_bf[ee, sl])
                            b.dma("qgp", l2, reads=[("w2s", ee, sl)],
                                  writes=[("we2b", par)], tag=f"fl2{call}{q}")
                        for c2 in range(2):
                            for ht in range(8):
                                bank = ht % 3
                                for kk in range(8):
                                    def m1(e, ht=ht, kk=kk, c2=c2, par=par,
                                           bank=bank):
                                        return e.matmul(
                                            pA(bank),
                                            we1b.ap()[:, par, kk,
                                                      ht * 128:(ht + 1) * 128],
                                            fxnt.ap()[:, kk, c2 * 512:
                                                      (c2 + 1) * 512],
                                            start=(kk == 0), stop=(kk == 7))
                                    b.op("tensor", m1,
                                         reads=[("we1b", par), fxnt.k(kk),
                                                ("psA", bank)],
                                         writes=[("psA", bank)],
                                         tag=f"fm1{call}")
                                def es(e, ht=ht, ee=ee, sl=sl, bank=bank):
                                    return e.activation(
                                        hb.ap()[:, ht, :], pA(bank), AF.Silu,
                                        bias=P["be1T"].ap()[:, ee * 32 + sl * 8
                                                            + ht:ee * 32
                                                            + sl * 8 + ht + 1])
                                b.op("scalar", es, reads=[("psA", bank)],
                                     sreads=[P["be1T"].k()],
                                     writes=[("hb", ht)], tag=f"fes{call}")
                            for mt2 in range(4):
                                mtg = c2 * 4 + mt2
                                for nd in range(2):
                                    bank = (mt2 * 2 + nd) % 2
                                    for kk in range(7, -1, -1):
                                        def m2(e, mt2=mt2, nd=nd, kk=kk,
                                               par=par, bank=bank):
                                            return e.matmul(
                                                pB(bank),
                                                hb.ap()[:, kk, mt2 * 128:
                                                        (mt2 + 1) * 128],
                                                we2b.ap()[:, par, kk,
                                                          nd * 512:
                                                          (nd + 1) * 512],
                                                start=(kk == 7),
                                                stop=(kk == 0))
                                        b.op("tensor", m2,
                                             reads=[("hb", kk), ("we2b", par),
                                                    ("psB", bank)],
                                             writes=[("psB", bank)],
                                             tag=f"fm2{call}")
                                    def ev(e, mtg=mtg, nd=nd, ee=ee, bank=bank):
                                        return e.scalar_tensor_tensor(
                                            fxacc.ap()[:, mtg, nd * 512:
                                                       (nd + 1) * 512],
                                            pB(bank),
                                            wnat.ap()[:, mtg, ee:ee + 1],
                                            fxacc.ap()[:, mtg, nd * 512:
                                                       (nd + 1) * 512],
                                            ALU.mult, ALU.add)
                                    b.op("vector", ev,
                                         reads=[("psB", bank), fxacc.k(mtg)],
                                         sreads=[wnat.k(mtg)],
                                         writes=[fxacc.k(mtg)],
                                         tag=f"fev{call}")

                last = (call == 1)
                for mt in range(8):
                    def stq(e, q0=q0, mt=mt, last=last):
                        dst = out if last else x_state
                        return e.dma_start(
                            out=dst[q0 + mt * 128: q0 + (mt + 1) * 128, :],
                            in_=fxacc.ap()[:, mt, :])
                    b.dma("qsync", stq, reads=[fxacc.k(mt)],
                          writes=[("xst", (q0 + mt * 128) // CH,
                                   (mt % 4))] if not last else [],
                          tag=f"fst{call}{q}")
            mem.release(m0)

        def attn():
            m0 = mem.mark()
            vbuf = mem.alloc("vbuf", [128, 32, 16, 65], BF16)
            b.op("vector", lambda e: e.memset(
                vbuf.ap()[:, :, :, 64:65], 1.0),
                writes=[("vb1",)], tag="vones")

            m1m = mem.mark()
            awqk = mem.alloc("awqk", [128, 2, 8, 1024], BF16)
            axnt = mem.alloc("axnt", [128, 8, 512], BF16)
            ax = mem.alloc("ax", [128, 1024], F32)
            axn32 = mem.alloc("axn32", [128, 1024], F32)
            axnb = mem.alloc("axnb", [128, 1024], BF16)
            aq32 = mem.alloc("aq32", [128, 1024], F32)
            ak32 = mem.alloc("ak32", [128, 1024], F32)
            aro = mem.alloc("aro", [128, 1024], F32)
            arp = mem.alloc("arp", [128, 1024], F32)
            aqb = mem.alloc("aqb", [128, 1024], BF16)
            akb = mem.alloc("akb", [128, 1024], BF16)
            acos = mem.alloc("acos", [128, 512], F32)
            asin = mem.alloc("asin", [128, 512], F32)
            aqt = mem.alloc("aqt", [128, 8, 128], BF16)
            akt = mem.alloc("akt", [128, 8, 128], BF16)
            sta = mem.alloc("sta", [128, 1, 4], F32)
            sdump = mem.alloc("sdumpa", [128, 1024], F32)

            def rope(src, srck, dst, dstk, tag):
                sv = src.ap().rearrange("p (i two) -> p i two", two=2)
                dv = dst.ap().rearrange("p (i two) -> p i two", two=2)
                b.op("vector", lambda e: e.tensor_mul(
                    arp.ap()[:, 0:512], sv[:, :, 1], asin.ap()),
                    reads=[srck, asin.k()], writes=[("arp", 0)], tag="r1" + tag)
                b.op("vector", lambda e: e.tensor_mul(
                    arp.ap()[:, 512:1024], sv[:, :, 0], acos.ap()),
                    reads=[srck, acos.k()], writes=[("arp", 1)], tag="r2" + tag)
                b.op("vector", lambda e: e.tensor_sub(
                    dv[:, :, 0], arp.ap()[:, 512:1024], arp.ap()[:, 0:512]),
                    reads=[("arp", 0), ("arp", 1)], writes=[dstk], tag="r3" + tag)
                b.op("vector", lambda e: e.tensor_mul(
                    arp.ap()[:, 0:512], sv[:, :, 0], asin.ap()),
                    reads=[srck, asin.k()], writes=[("arp", 0)], tag="r4" + tag)
                b.op("vector", lambda e: e.tensor_mul(
                    arp.ap()[:, 512:1024], sv[:, :, 1], acos.ap()),
                    reads=[srck, acos.k()], writes=[("arp", 1)], tag="r5" + tag)
                b.op("vector", lambda e: e.tensor_add(
                    dv[:, :, 1], arp.ap()[:, 512:1024], arp.ap()[:, 0:512]),
                    reads=[("arp", 0), ("arp", 1), dstk], writes=[dstk],
                    tag="r6" + tag)

            for c in range(NCH):
                for wi, wn in ((0, "q"), (1, "k")):
                    def ldw(e, wi=wi, wn=wn):
                        return e.dma_start(
                            out=awqk.ap()[:, wi],
                            in_=wT_scr[wn].rearrange("k p d -> p k d"))
                    b.dma("qgp", ldw, writes=[("awqk", wi)], tag=f"aldw{c}{wn}")
                for tt in range(4):
                    gt = c * 4 + tt
                    def ldx(e, gt=gt):
                        return e.dma_start(
                            out=ax.ap(),
                            in_=x_state[gt * 128:(gt + 1) * 128, :])
                    b.dma("qsync", ldx,
                          reads=[("xst", gt // 4, gt % 4)],
                          writes=[ax.k()], tag=f"aldx{c}{tt}")
                    def ldcs(e, gt=gt):
                        return e.dma_start(out=acos.ap(),
                                           in_=cos_t[gt * 128:(gt + 1) * 128, :])
                    b.dma("qsync", ldcs, writes=[acos.k()], tag=f"acos{c}{tt}")
                    def ldsn(e, gt=gt):
                        return e.dma_start(out=asin.ap(),
                                           in_=sin_t[gt * 128:(gt + 1) * 128, :])
                    b.dma("qsync", ldsn, writes=[asin.k()], tag=f"asin{c}{tt}")

                    negmu, rstd, kng, krs = ln_stats(
                        ax.ap(), [ax.k()], sta, 0, sdump, f"at{c}{tt}")
                    b.op("vector", lambda e, negmu=negmu, rstd=rstd:
                         e.tensor_scalar(axn32.ap(), ax.ap(), negmu, rstd,
                                         ALU.add, ALU.mult),
                         reads=[ax.k()], sreads=[kng, krs], writes=[axn32.k()],
                         tag=f"axn{c}{tt}")
                    b.op("vector", lambda e: e.tensor_copy(axnb.ap(), axn32.ap()),
                         reads=[axn32.k()], writes=[axnb.k()], tag=f"axb{c}{tt}")
                    for dk in range(8):
                        slot = dk % 8
                        b.op("tensor", lambda e, dk=dk, slot=slot: e.transpose(
                            pT(slot), axnb.ap()[:, dk * 128:(dk + 1) * 128],
                            P["idtb"].ap()),
                            reads=[axnb.k(), P["idtb"].k()], writes=[pTk(slot)],
                            tag=f"atr{c}")
                        b.op("vector", lambda e, dk=dk, tt=tt, slot=slot:
                             e.tensor_copy(
                                 axnt.ap()[:, dk, tt * 128:(tt + 1) * 128],
                                 pT(slot)),
                             reads=[pTk(slot)], writes=[axnt.k(dk)],
                             tag=f"ate{c}")
                    for wi, dst, dstk in ((0, aq32, aq32.k()),
                                          (1, ak32, ak32.k())):
                        for nd in range(2):
                            bank = nd % 2
                            for kk in range(8):
                                def mq(e, wi=wi, nd=nd, kk=kk, tt=tt, bank=bank):
                                    return e.matmul(
                                        pA(bank),
                                        axnt.ap()[:, kk, tt * 128:(tt + 1) * 128],
                                        awqk.ap()[:, wi, kk,
                                                  nd * 512:(nd + 1) * 512],
                                        start=(kk == 0), stop=(kk == 7))
                                b.op("tensor", mq,
                                     reads=[axnt.k(kk), ("awqk", wi),
                                            ("psA", bank)],
                                     writes=[("psA", bank)], tag=f"amq{c}")
                            def eq_(e, dst=dst, nd=nd, bank=bank):
                                return e.tensor_copy(
                                    dst.ap()[:, nd * 512:(nd + 1) * 512],
                                    pA(bank))
                            b.op("vector", eq_, reads=[("psA", bank)],
                                 writes=[dstk], tag=f"aeq{c}")
                    rope(aq32, aq32.k(), aro, aro.k(), f"q{c}{tt}")
                    b.op("vector", lambda e: e.tensor_copy(aqb.ap(), aro.ap()),
                         reads=[aro.k()], writes=[aqb.k()], tag=f"aqb{c}")
                    rope(ak32, ak32.k(), aro, aro.k(), f"k{c}{tt}")
                    b.op("vector", lambda e: e.tensor_copy(akb.ap(), aro.ap()),
                         reads=[aro.k()], writes=[akb.k()], tag=f"akb{c}")
                    for src, dst, dstk, scr in ((aqb, aqt, "aqt", q_scr),
                                                (akb, akt, "akt", k_scr)):
                        for dj in range(8):
                            slot = dj % 8
                            b.op("tensor", lambda e, src=src, dj=dj, slot=slot:
                                 e.transpose(
                                     pT(slot),
                                     src.ap()[:, dj * 128:(dj + 1) * 128],
                                     P["idtb"].ap()),
                                 reads=[src.k(), P["idtb"].k()],
                                 writes=[pTk(slot)], tag=f"aqtr{c}")
                            b.op("vector", lambda e, dst=dst, dj=dj, slot=slot:
                                 e.tensor_copy(dst.ap()[:, dj, :], pT(slot)),
                                 reads=[pTk(slot)], writes=[(dstk,)],
                                 tag=f"aqte{c}")
                        def stq(e, scr=scr, c=c, tt=tt, dst=dst):
                            return e.dma_start(
                                out=scr[c, :, :, tt * 128:(tt + 1) * 128]
                                .rearrange("j p t -> p j t"),
                                in_=dst.ap())
                        b.dma("qsync", stq, reads=[(dstk,)],
                              writes=[("qk_scr", dstk, c)], tag=f"aqst{c}")

                def ldv(e):
                    return e.dma_start(
                        out=awqk.ap()[:, 0],
                        in_=wT_scr["v"].rearrange("k p d -> p k d"))
                b.dma("qgp", ldv, writes=[("awqk", 0)], tag=f"aldv{c}")
                for tt in range(4):
                    gt = c * 4 + tt
                    for nd in range(2):
                        bank = nd % 2
                        for kk in range(8):
                            def mv(e, nd=nd, kk=kk, tt=tt, bank=bank):
                                return e.matmul(
                                    pA(bank),
                                    axnt.ap()[:, kk, tt * 128:(tt + 1) * 128],
                                    awqk.ap()[:, 0, kk, nd * 512:(nd + 1) * 512],
                                    start=(kk == 0), stop=(kk == 7))
                            b.op("tensor", mv,
                                 reads=[axnt.k(kk), ("awqk", 0), ("psA", bank)],
                                 writes=[("psA", bank)], tag=f"amv{c}")
                        def evv(e, gt=gt, nd=nd, bank=bank):
                            return e.tensor_copy(
                                vbuf.ap()[:, gt, nd * 8:(nd + 1) * 8, 0:64],
                                pA(bank).rearrange("p (h d) -> p h d", d=64))
                        b.op("vector", evv, reads=[("psA", bank)],
                             writes=[("vb", gt)], tag=f"aev{c}")
            mem.release(m1m)
            b.barrier("attn_mid")

            awo = mem.alloc("awo", [128, 8, 1024], BF16)
            qtc = mem.alloc("qtc", [128, 8, 512], BF16)
            ktb = mem.alloc("ktb", [128, 4096], BF16)
            pbuf = mem.alloc("pbuf", [128, 32, 512], BF16)
            aoT = mem.alloc("aoT", [128, 8, 512], BF16)
            rc1 = mem.alloc("rc1", [1, 512], F32)
            rcb = mem.alloc("rcb", [64, 512], F32)
            wostg = mem.alloc("awostg", [128, 2, 512], F32)

            def ldwo(e):
                return e.dma_start(out=awo.ap(),
                                   in_=wT_scr["o"].rearrange("k p d -> p k d"))
            b.dma("qgp", ldwo, writes=[awo.k()], tag="aldo")

            for c in range(NCH):
                def ldq(e, c=c):
                    return e.dma_start(
                        out=qtc.ap(),
                        in_=q_scr[c].rearrange("j p t -> p j t"))
                b.dma("qsync", ldq, reads=[("qk_scr", "aqt", c)],
                      writes=[qtc.k()], tag=f"pldq{c}")
                for j in range(8):
                    def ldk(e, j=j):
                        return e.dma_start(
                            out=ktb.ap().rearrange("p (c t) -> p c t", t=512),
                            in_=k_scr[:, j].rearrange("c p t -> p c t"))
                    b.dma("qsync", ldk,
                          reads=[("qk_scr", "akt", cc) for cc in range(NCH)],
                          writes=[ktb.k()], tag=f"pldk{c}{j}")
                    for h2 in range(2):
                        h = 2 * j + h2
                        for pg in range(16):
                            b0 = (pg % 2) * 2
                            for ti in range(2):
                                tkt = pg * 2 + ti
                                def ms(e, j=j, h2=h2, tkt=tkt, ti=ti, b0=b0):
                                    return e.matmul(
                                        pA(b0 + ti),
                                        ktb.ap()[h2 * 64:(h2 + 1) * 64,
                                                 tkt * 128:(tkt + 1) * 128],
                                        qtc.ap()[h2 * 64:(h2 + 1) * 64, j, :],
                                        start=True, stop=True,
                                        tile_position=(h2 * 64, 0))
                                b.op("tensor", ms,
                                     reads=[ktb.k(), qtc.k(), ("psA", b0 + ti)],
                                     writes=[("psA", b0 + ti)], tag=f"pms{c}")
                            def pex(e, pg=pg, b0=b0):
                                return e.activation(
                                    pbuf.ap()[:, pg * 2:(pg + 1) * 2, :],
                                    psA[:, b0 * 512:(b0 + 2) * 512],
                                    AF.Exp, scale=P["cst"].ap()[:, 2:3])
                            b.op("scalar", pex,
                                 reads=[("psA", b0), ("psA", b0 + 1)],
                                 writes=[("pb", pg)], tag=f"pex{c}")
                        for tkt in range(31, -1, -1):
                            def mav(e, h=h, tkt=tkt):
                                return e.matmul(
                                    psB[0:65, 0:512],
                                    vbuf.ap()[:, tkt, h, :],
                                    pbuf.ap()[:, tkt, :],
                                    start=(tkt == 31), stop=(tkt == 0))
                            b.op("tensor", mav,
                                 reads=[("vb", tkt), ("vb1",), ("pb", tkt // 2),
                                        ("psB", 0)],
                                 writes=[("psB", 0)], tag=f"pav{c}")
                        b.op("vector", lambda e: e.reciprocal(
                            rc1.ap(), psB[64:65, 0:512]),
                            reads=[("psB", 0)], writes=[rc1.k()], tag=f"prc{c}")
                        def brc(e):
                            return e.matmul(psB[0:64, 512:1024],
                                            P["ones1"].ap()[0:1, 0:64],
                                            rc1.ap(), start=True, stop=True)
                        b.op("tensor", brc,
                             reads=[rc1.k(), P["ones1"].k(), ("psB", 1)],
                             writes=[("psB", 1)], tag=f"pbrc{c}")
                        b.op("vector", lambda e: e.tensor_copy(
                            rcb.ap(), psB[0:64, 512:1024]),
                            reads=[("psB", 1)], writes=[rcb.k()], tag=f"prcb{c}")
                        def cmp_(e, j=j, h2=h2):
                            return e.tensor_mul(
                                aoT.ap()[h2 * 64:(h2 + 1) * 64, j, :],
                                psB[0:64, 0:512], rcb.ap())
                        b.op("vector", cmp_,
                             reads=[("psB", 0), rcb.k(), aoT.k()],
                             writes=[aoT.k()], tag=f"pcm{c}")
                for mt in range(4):
                    for nd in range(2):
                        bank = nd % 2
                        for kk in range(8):
                            def mo(e, mt=mt, nd=nd, kk=kk, bank=bank):
                                return e.matmul(
                                    pA(bank),
                                    aoT.ap()[:, kk, mt * 128:(mt + 1) * 128],
                                    awo.ap()[:, kk, nd * 512:(nd + 1) * 512],
                                    start=(kk == 0), stop=(kk == 7))
                            b.op("tensor", mo,
                                 reads=[aoT.k(), awo.k(), ("psA", bank)],
                                 writes=[("psA", bank)], tag=f"pmo{c}")
                        sp = (mt * 2 + nd) % 2
                        def oev(e, bank=bank, sp=sp):
                            return e.tensor_copy(wostg.ap()[:, sp, :], pA(bank))
                        b.op("vector", oev, reads=[("psA", bank)],
                             writes=[wostg.k(sp)], tag=f"poev{c}")
                        def oacc(e, c=c, mt=mt, nd=nd, sp=sp):
                            return e.dma_start(
                                out=x_state[c * CH + mt * 128:
                                            c * CH + (mt + 1) * 128,
                                            nd * 512:(nd + 1) * 512],
                                in_=wostg.ap()[:, sp, :], accum_op=ALU.add)
                        b.dma("qgp", oacc, reads=[wostg.k(sp)],
                              writes=[("xst", c, mt)], tag=f"poac{c}")
            mem.release(m0)

        if "setup" in phases:
            setup()
            b.barrier("p_setup")
        if "mamba" in phases:
            mamba()
            b.barrier("p_mamba")
        if "ffn1" in phases:
            ffn(0)
            b.barrier("p_ffn1")
        if "attn" in phases:
            attn()
            b.barrier("p_attn")
        if "ffn2" in phases:
            ffn(1)
            b.barrier("p_ffn2")
        if "ffn2" not in phases:
            b.dma("qsync", lambda e: e.dma_start(out=out, in_=x_state),
                  tag="cp_out")
        b.finish("gpsimd")
        b.emit(ctx)
    return nc


def kernel(**inputs):
    nc = build_program()
    in_maps = []
    full = {n: np.ascontiguousarray(inputs[n], dtype=np.float32)
            for n in INPUT_NAMES}
    cos_np, sin_np = _rope_tables()
    full["c_cos"] = cos_np
    full["c_sin"] = sin_np
    for core in range(8):
        m = dict(full)
        m["L0_attn_a_x"] = np.ascontiguousarray(full["L0_attn_a_x"][core])
        in_maps.append(m)
    res = run_bass_kernel_spmd(nc, in_maps, core_ids=list(range(8)))
    return np.stack([r["out"] for r in res.results]).astype(np.float32)


# version 03
# speedup vs baseline: 1.0620x; 1.0620x over previous
import math
from contextlib import ExitStack

import numpy as np
import ml_dtypes

import concourse.bass as bass
import concourse.mybir as mybir
from concourse.bass_utils import run_bass_kernel_spmd

F32 = mybir.dt.float32
BF16 = mybir.dt.bfloat16
AF = mybir.ActivationFunctionType
ALU = mybir.AluOpType
AX = mybir.AxisListType

T = 4096
DM = 1024
DI = 2048
DST = 16
DTR = 64
NE = 8
HF = 4096
NH = 16
DKH = 64
CH = 512
NCH = T // CH
LNEPS = 1e-5

_PHASES = ("setup", "mamba", "ffn1", "attn", "ffn2")
MSTAGE = 99
MCHUNKS = NCH

ENGINES = ("tensor", "vector", "scalar", "gpsimd", "sync")
QUEUES = {"qsync": "sync", "qsca": "scalar", "qgp": "gpsimd"}
QLANES = 6
LANE_DOMAINS = tuple(f"{q}{l}" for q in QUEUES for l in range(QLANES))
DOMAINS = ENGINES + LANE_DOMAINS


class _Op:
    __slots__ = ("eng", "domain", "fn", "deps", "inc_needed", "inc_val",
                 "waits", "clock", "tag")

    def __init__(self, eng, domain, fn, tag):
        self.eng = eng
        self.domain = domain
        self.fn = fn
        self.deps = []
        self.inc_needed = False
        self.inc_val = 0
        self.waits = []
        self.clock = None
        self.tag = tag


class Builder:
    def __init__(self, nc):
        self.nc = nc
        self.ops = []
        self.by_eng = {e: [] for e in ENGINES}
        self.last_in_domain = {d: None for d in DOMAINS}
        self.last_write = {}
        self.readers = {}
        self.same_engine_edges = {"tensor": False, "vector": True,
                                  "scalar": True, "gpsimd": True, "sync": False}
        self.sems = {}
        self.qcount = {q: 0 for q in QUEUES}

    def _dep(self, op, producer, kind="raw"):
        if producer is None or producer is op:
            return
        if producer.eng == op.eng and producer.domain in ENGINES:
            if kind in ("war", "waw"):
                return
            if kind == "raw" and not self.same_engine_edges[op.eng]:
                return
        if producer not in op.deps:
            op.deps.append(producer)

    def _record(self, op, reads, writes, sreads):
        for k in reads:
            self._dep(op, self.last_write.get(k), "raw")
        for k in sreads:
            self._dep(op, self.last_write.get(k), "sraw")
        for k in writes:
            self._dep(op, self.last_write.get(k), "waw")
            for r in self.readers.get(k, ()):
                self._dep(op, r, "war")
        for k in reads:
            self.readers.setdefault(k, []).append(op)
        for k in sreads:
            self.readers.setdefault(k, []).append(op)
        for k in writes:
            self.last_write[k] = op
            self.readers[k] = []
        self.ops.append(op)
        self.by_eng[op.eng].append(op)
        self.last_in_domain[op.domain] = op
        return op

    def op(self, eng, fn, reads=(), writes=(), sreads=(), tag=""):
        return self._record(_Op(eng, eng, fn, tag), reads, writes, sreads)

    def dma(self, queue, fn, reads=(), writes=(), tag=""):
        lane = self.qcount[queue] % QLANES
        self.qcount[queue] += 1
        dom = f"{queue}{lane}"
        op = _Op(QUEUES[queue], dom, fn, tag)
        self._dep(op, self.last_in_domain[dom], "sraw")
        return self._record(op, reads, writes, ())

    def barrier(self, tag="barrier"):
        lasts = [o for o in self.last_in_domain.values() if o is not None]
        for e in ENGINES:
            o = _Op(e, e, lambda eng: eng.nop(), tag)
            for p in lasts:
                self._dep(o, p, "sraw")
            self._record(o, (), (), ())

    def finish(self, eng="gpsimd", tag="finish"):
        lasts = [o for o in self.last_in_domain.values() if o is not None]
        o = _Op(eng, eng, lambda e: e.nop(), tag)
        for p in lasts:
            self._dep(o, p, "sraw")
        self._record(o, (), (), ())

    def emit(self, ctx):
        nc = self.nc
        for p in self.ops:
            for d in p.deps:
                d.inc_needed = True
        counters = {d: 0 for d in DOMAINS}
        for o in self.ops:
            if o.domain not in ENGINES:
                o.inc_needed = True
            if o.inc_needed:
                counters[o.domain] += 1 if o.domain in ENGINES else 16
            o.inc_val = counters[o.domain]
        lastclock = {e: {d: 0 for d in DOMAINS} for e in ENGINES}
        own_waited = {e: 0 for e in ENGINES}
        for o in self.ops:
            clk = dict(lastclock[o.eng])
            waits = []
            best = {}
            for p in o.deps:
                q = best.get(p.domain)
                if q is None or p.inc_val > q.inc_val:
                    best[p.domain] = p
            for p in sorted(best.values(), key=lambda p: (p.domain, p.inc_val)):
                if p.domain == o.eng:
                    if own_waited[o.eng] < p.inc_val:
                        waits.append((p.domain, p.inc_val))
                        own_waited[o.eng] = p.inc_val
                    if clk[p.domain] < p.inc_val:
                        clk[p.domain] = p.inc_val
                elif clk[p.domain] < p.inc_val:
                    waits.append((p.domain, p.inc_val))
                    clk[p.domain] = p.inc_val
                    for d, v in p.clock.items():
                        if clk[d] < v:
                            clk[d] = v
            if o.inc_needed and o.domain in ENGINES and clk[o.domain] < o.inc_val:
                clk[o.domain] = o.inc_val
            o.waits = waits
            o.clock = clk
            lastclock[o.eng] = clk
        for d in DOMAINS:
            if counters[d] > 0:
                self.sems[d] = ctx.enter_context(nc.semaphore(f"sem_{d}"))
        block = ctx.enter_context(nc.Block())
        builder = self

        def run_engine(engine_obj, ename):
            for o in builder.by_eng[ename]:
                for d, v in o.waits:
                    engine_obj.wait_ge(builder.sems[d], v)
                inst = o.fn(engine_obj)
                if o.inc_needed:
                    inst.then_inc(builder.sems[o.domain],
                                  1 if o.domain in ENGINES else 16)

        for ename in ENGINES:
            if self.by_eng[ename]:
                getattr(block, ename)(lambda e, _n=ename: run_engine(e, _n))

    def stats(self):
        return {"ops": len(self.ops),
                "waits": sum(len(o.waits) for o in self.ops),
                "per_engine": {e: len(v) for e, v in self.by_eng.items()}}


class _View:
    __slots__ = ("mem", "name", "off", "shape", "dtype")

    def __init__(self, mem, name, off, shape, dtype):
        self.mem = mem
        self.name = name
        self.off = off
        self.shape = shape
        self.dtype = dtype

    def ap(self):
        esz = mybir.dt.size(self.dtype)
        elems = math.prod(self.shape[1:])
        base = self.mem.handle.ap().bitcast(self.dtype)
        v = base[0:self.shape[0], self.off // esz: self.off // esz + elems]
        if len(self.shape) > 2:
            dims = " ".join(f"d{i}" for i in range(1, len(self.shape)))
            kw = {f"d{i}": s for i, s in enumerate(self.shape[1:], 1)}
            v = v.rearrange(f"p ({dims}) -> p {dims}", **kw)
        return v

    def k(self, *sub):
        return (self.name,) + sub


class Mem:
    def __init__(self, nc, cap_bytes_per_partition):
        self.nc = nc
        self.cap = cap_bytes_per_partition
        self.off = 0
        self.peak = 0
        self.handle = None

    def alloc(self, name, shape, dtype, align=64):
        nbytes = math.prod(shape[1:]) * mybir.dt.size(dtype)
        self.off = (self.off + align - 1) // align * align
        off = self.off
        self.off += nbytes
        self.peak = max(self.peak, self.off)
        assert self.off <= self.cap, \
            f"SBUF arena overflow at {name}: {self.off} > {self.cap}"
        return _View(self, name, off, shape, dtype)

    def alias(self, view, shape, dtype):
        nbytes = math.prod(shape[1:]) * mybir.dt.size(dtype)
        avail = math.prod(view.shape[1:]) * mybir.dt.size(view.dtype)
        assert nbytes <= avail, f"alias too big on {view.name}"
        return _View(self, view.name, view.off, shape, dtype)

    def mark(self):
        return self.off

    def release(self, m):
        self.off = m

    def materialize(self, ctx):
        self.handle = ctx.enter_context(
            self.nc.sbuf_tensor("arena", [128, self.cap], mybir.dt.uint8))


INPUT_NAMES = [
    "L0_attn_a_x", "L0_attn_a_ln_g", "L0_attn_a_ln_b", "L0_attn_a_W_in",
    "L0_attn_a_conv_w", "L0_attn_a_conv_b", "L0_attn_a_W_xp", "L0_attn_a_W_dt",
    "L0_attn_a_b_dt", "L0_attn_a_A_log", "L0_attn_a_D", "L0_attn_a_W_out",
    "L0_attn_b_W_q", "L0_attn_b_W_k", "L0_attn_b_W_v", "L0_attn_b_W_o",
    "L0_ffn_rw1", "L0_ffn_rb1", "L0_ffn_rw2", "L0_ffn_rb2",
    "L0_ffn_we1", "L0_ffn_be1", "L0_ffn_we2", "L0_ffn_be2",
]

SHAPES = {
    "L0_attn_a_x": [T, DM], "L0_attn_a_ln_g": [DM], "L0_attn_a_ln_b": [DM],
    "L0_attn_a_W_in": [2 * DI, DM], "L0_attn_a_conv_w": [DI, 4],
    "L0_attn_a_conv_b": [DI], "L0_attn_a_W_xp": [96, DI],
    "L0_attn_a_W_dt": [DI, DTR], "L0_attn_a_b_dt": [DI],
    "L0_attn_a_A_log": [DI, DST], "L0_attn_a_D": [DI],
    "L0_attn_a_W_out": [DM, DI],
    "L0_attn_b_W_q": [DM, DM], "L0_attn_b_W_k": [DM, DM],
    "L0_attn_b_W_v": [DM, DM], "L0_attn_b_W_o": [DM, DM],
    "L0_ffn_rw1": [DM, DM], "L0_ffn_rb1": [DM], "L0_ffn_rw2": [DM, NE],
    "L0_ffn_rb2": [NE], "L0_ffn_we1": [NE, DM, HF], "L0_ffn_be1": [NE, HF],
    "L0_ffn_we2": [NE, HF, DM], "L0_ffn_be2": [NE, DM],
}


def _rope_tables():
    inv = 1.0 / (10000.0 ** (np.arange(0, DKH, 2, dtype=np.float64) / DKH))
    phase = np.arange(T, dtype=np.float64)[:, None] * inv[None, :]
    cos = np.tile(np.cos(phase), (1, NH)).astype(np.float32)
    sin = np.tile(np.sin(phase), (1, NH)).astype(np.float32)
    return cos, sin


def build_program(phases=None):
    phases = _PHASES if phases is None else phases
    nc = bass.Bass("TRN2", target_bir_lowering=False, debug=False)

    di = {n: nc.dram_tensor(n, SHAPES[n], F32, kind="ExternalInput").ap()
          for n in INPUT_NAMES}
    out = nc.dram_tensor("out", [T, DM], F32, kind="ExternalOutput").ap()

    x_state = nc.dram_tensor("x_state", [T, DM], F32, kind="Internal").ap()
    z_scr = nc.dram_tensor("z_scr", [NCH, 16, 128, CH], F32, kind="Internal").ap()
    uc_scr = nc.dram_tensor("uc_scr", [NCH, 16, 128, CH], F32, kind="Internal").ap()
    winT_scr = nc.dram_tensor("winT_scr", [8, 128, 2 * DI], BF16, kind="Internal").ap()
    woutT_scr = nc.dram_tensor("woutT_scr", [16, 128, DM], BF16, kind="Internal").ap()
    wT_scr = {w: nc.dram_tensor(f"w{w}T_scr", [8, 128, DM], BF16,
                                kind="Internal").ap()
              for w in ("q", "k", "v", "o")}
    we1_bf = nc.dram_tensor("we1_bf", [NE, 4, 128, 8192], BF16, kind="Internal").ap()
    we2_bf = nc.dram_tensor("we2_bf", [NE, 4, 128, 8192], BF16, kind="Internal").ap()
    q_scr = nc.dram_tensor("q_scr", [NCH, 8, 128, CH], BF16, kind="Internal").ap()
    k_scr = nc.dram_tensor("k_scr", [NCH, 8, 128, CH], BF16, kind="Internal").ap()

    idt_bf_t = nc.inline_tensor(np.eye(128, dtype=ml_dtypes.bfloat16), "c_idtbf").ap()
    idt_f_t = nc.inline_tensor(np.eye(128, dtype=np.float32), "c_idtf").ap()
    oneh_np = np.zeros((16, 16 * 128), dtype=np.float32)
    for n in range(16):
        oneh_np[n, n * 128:(n + 1) * 128] = 1.0
    oneh_t = nc.inline_tensor(oneh_np, "c_oneh").ap()
    ones1_t = nc.inline_tensor(np.ones((1, 128), dtype=np.float32), "c_ones1").ap()
    cos_t = nc.dram_tensor("c_cos", [T, 512], F32, kind="ExternalInput").ap()
    sin_t = nc.dram_tensor("c_sin", [T, 512], F32, kind="ExternalInput").ap()

    b = Builder(nc)
    mem = Mem(nc, 206 * 1024)

    P = {}
    P["idtb"] = mem.alloc("idtb", [128, 128], BF16)
    P["idtf"] = mem.alloc("idtf", [128, 128], F32)
    P["oneh"] = mem.alloc("oneh", [16, 16 * 128], F32)
    P["ones1"] = mem.alloc("ones1", [1, 128], F32)
    P["lng"] = mem.alloc("lng", [128, 8], F32)
    P["lnb"] = mem.alloc("lnb", [128, 8], F32)
    P["convw"] = mem.alloc("convw", [128, 16, 4], F32)
    P["convb"] = mem.alloc("convb", [128, 16], F32)
    P["bdt"] = mem.alloc("bdt", [128, 16], F32)
    P["Dv"] = mem.alloc("Dv", [128, 16], F32)
    P["A"] = mem.alloc("A", [128, 16, 16], F32)
    P["wxpT"] = mem.alloc("wxpT", [128, 16, 96], BF16)
    P["wdtT"] = mem.alloc("wdtT", [64, 16, 128], BF16)
    P["rb1T"] = mem.alloc("rb1T", [128, 8], F32)
    P["rb2bc"] = mem.alloc("rb2bc", [128, 8], F32)
    P["be1T"] = mem.alloc("be1T", [128, 256], F32)
    P["be2n"] = mem.alloc("be2n", [8, 1024], F32)
    P["rw2"] = mem.alloc("rw2", [128, 8, 8], BF16)
    P["hst"] = mem.alloc("hst", [128, 256], F32)
    P["cst"] = mem.alloc("cst", [128, 4], F32)

    with ExitStack() as ctx:
        mem.materialize(ctx)
        psA = ctx.enter_context(nc.psum_tensor("psA", [128, 2048], F32))
        psB = ctx.enter_context(nc.psum_tensor("psB", [128, 1024], F32))
        psT = ctx.enter_context(nc.psum_tensor("psT", [128, 2048], BF16))

        def pA(i):
            return psA[:, i * 512:(i + 1) * 512]

        def pB(i):
            return psB[:, i * 512:(i + 1) * 512]

        def pT(i):
            return psT[:, (i % 2) * 1024:(i % 2) * 1024 + 128]

        def pTk(i):
            return ("psT", i % 2)

        def trbf(src_ap, src_keys, dst_ap, dst_keys, slot, tag=""):
            b.op("tensor", lambda e, s=src_ap: e.transpose(
                pT(slot), s, P["idtb"].ap()),
                reads=list(src_keys) + [P["idtb"].k()], writes=[pTk(slot)],
                tag="tr" + tag)
            b.op("vector", lambda e, d=dst_ap: e.tensor_copy(d, pT(slot)),
                 reads=[pTk(slot)], writes=list(dst_keys), tag="te" + tag)

        def ln_stats(xap, xkeys, stv, i, sdump, tag):
            negmu = stv.ap()[:, i, 0:1]
            rstd = stv.ap()[:, i, 1:2]
            s_ = stv.ap()[:, i, 2:3]
            q_ = stv.ap()[:, i, 3:4]
            kneg, krs, ks, kq = stv.k(i, 0), stv.k(i, 1), stv.k(i, 2), stv.k(i, 3)
            b.op("vector", lambda e: e.reduce_sum(s_, xap, AX.X),
                 reads=list(xkeys), writes=[ks], tag="lns" + tag)
            b.op("scalar", lambda e: e.activation(
                sdump.ap(), xap, AF.Square, accum_out=q_),
                reads=list(xkeys), writes=[kq, sdump.k()], tag="lnq" + tag)
            b.op("vector", lambda e: e.tensor_scalar_mul(negmu, s_, -1.0 / DM),
                 sreads=[ks], writes=[kneg], tag="lnm" + tag)
            b.op("vector", lambda e: e.tensor_scalar_mul(rstd, q_, 1.0 / DM),
                 sreads=[kq], writes=[krs], tag="lnv" + tag)
            b.op("vector", lambda e: e.tensor_mul(s_, negmu, negmu),
                 reads=[kneg], writes=[ks], tag="lnv2" + tag)
            b.op("vector", lambda e: e.tensor_sub(rstd, rstd, s_),
                 reads=[krs, ks], writes=[krs], tag="lnv3" + tag)
            b.op("scalar", lambda e: e.activation(
                rstd, rstd, AF.Ln, bias=P["cst"].ap()[:, 0:1]),
                reads=[krs], sreads=[P["cst"].k()], writes=[krs],
                tag="lnv4" + tag)
            b.op("scalar", lambda e: e.activation(
                rstd, rstd, AF.Exp, scale=P["cst"].ap()[:, 1:2]),
                reads=[krs], sreads=[P["cst"].k()], writes=[krs],
                tag="lnv5" + tag)
            return negmu, rstd, kneg, krs

        def setup():
            m0 = mem.mark()
            b.op("vector", lambda e: e.memset(P["cst"].ap()[:, 0:1],
                                              float(LNEPS)),
                 writes=[P["cst"].k()], tag="c0")
            b.op("vector", lambda e: e.memset(P["cst"].ap()[:, 1:2], -0.5),
                 reads=[P["cst"].k()], writes=[P["cst"].k()], tag="c1")
            b.op("vector", lambda e: e.memset(P["cst"].ap()[:, 2:3], 0.125),
                 reads=[P["cst"].k()], writes=[P["cst"].k()], tag="c2")
            b.op("vector", lambda e: e.memset(P["cst"].ap()[:, 3:4], 1.0),
                 reads=[P["cst"].k()], writes=[P["cst"].k()], tag="c3")
            for nm, src in [("idtb", idt_bf_t), ("idtf", idt_f_t),
                            ("oneh", oneh_t), ("ones1", ones1_t)]:
                b.dma("qsync", lambda e, d=P[nm].ap(), s=src: e.dma_start(
                    out=d, in_=s), writes=[P[nm].k()], tag="ld_" + nm)

            for piece in range(4):
                def ldc(e, piece=piece):
                    return e.dma_start(
                        out=P["convw"].ap()[:, piece * 4:(piece + 1) * 4, :],
                        in_=di["L0_attn_a_conv_w"][piece * 512:(piece + 1) * 512, :]
                        .rearrange("(t p) k -> p t k", p=128))
                b.dma("qsync", ldc, writes=[P["convw"].k()], tag="ld_convw")

            atmp = mem.alloc("atmp", [128, 16, 16], F32)
            for piece in range(4):
                def lda(e, piece=piece):
                    return e.dma_start(
                        out=atmp.ap()[:, piece * 4:(piece + 1) * 4, :],
                        in_=di["L0_attn_a_A_log"][piece * 512:(piece + 1) * 512, :]
                        .rearrange("(t p) n -> p t n", p=128))
                b.dma("qsync", lda, writes=[atmp.k()], tag="ld_alog")
            b.op("scalar", lambda e: e.activation(P["A"].ap(), atmp.ap(),
                                                  AF.Exp,
                                                  scale=P["cst"].ap()[:, 3:4]),
                 reads=[atmp.k()], sreads=[P["cst"].k()],
                 writes=[P["A"].k()], tag="expA")
            b.op("vector", lambda e: e.tensor_scalar_mul(
                P["A"].ap(), P["A"].ap(), -1.0),
                reads=[P["A"].k()], writes=[P["A"].k()], tag="negA")

            vtmp = mem.alloc("vtmp", [128, 128], F32)

            def vec_cols(src, n, dst, dtag):
                b.dma("qsync", lambda e: e.dma_start(
                    out=vtmp.ap()[0:n, :],
                    in_=src.rearrange("(r c) -> r c", c=128)),
                    writes=[vtmp.k()], tag="ldv" + dtag)
                b.op("tensor", lambda e: e.transpose(
                    psB[:, 0:n], vtmp.ap()[0:n, :], P["idtf"].ap()[0:n, 0:n]),
                    reads=[vtmp.k(), P["idtf"].k()], writes=[("psB", 0)],
                    tag="trv" + dtag)
                b.op("vector", lambda e: e.tensor_copy(dst.ap()[:, 0:n],
                                                       psB[:, 0:n]),
                     reads=[("psB", 0)], writes=[dst.k()], tag="evv" + dtag)

            vec_cols(di["L0_attn_a_ln_g"], 8, P["lng"], "lng")
            vec_cols(di["L0_attn_a_ln_b"], 8, P["lnb"], "lnb")
            vec_cols(di["L0_attn_a_conv_b"], 16, P["convb"], "convb")
            vec_cols(di["L0_attn_a_b_dt"], 16, P["bdt"], "bdt")
            vec_cols(di["L0_attn_a_D"], 16, P["Dv"], "Dv")
            vec_cols(di["L0_ffn_rb1"], 8, P["rb1T"], "rb1")

            btmp = mem.alloc("btmp", [128, 2, 128], F32)
            b.dma("qsync", lambda e: e.dma_start(
                out=btmp.ap(),
                in_=di["L0_ffn_be1"].rearrange("e (ht p) -> (e ht) p", p=128)
                .rearrange("(u r) p -> r u p", u=2)),
                writes=[btmp.k()], tag="ld_be1")
            for u in range(2):
                b.op("tensor", lambda e, u=u: e.transpose(
                    pB(u)[:, 0:128], btmp.ap()[:, u, :], P["idtf"].ap()),
                    reads=[btmp.k(), P["idtf"].k()], writes=[("psB", u)],
                    tag="tr_be1")
                b.op("vector", lambda e, u=u: e.tensor_copy(
                    P["be1T"].ap()[:, u * 128:(u + 1) * 128], pB(u)[:, 0:128]),
                    reads=[("psB", u)], writes=[P["be1T"].k()], tag="ev_be1")
            b.dma("qsync", lambda e: e.dma_start(
                out=P["be2n"].ap(), in_=di["L0_ffn_be2"]),
                writes=[P["be2n"].k()], tag="ld_be2n")

            r2tmp = mem.alloc("r2tmp", [1, 8], F32)
            b.dma("qsync", lambda e: e.dma_start(
                out=r2tmp.ap(),
                in_=di["L0_ffn_rb2"].rearrange("(a e) -> a e", a=1)),
                writes=[r2tmp.k()], tag="ld_rb2")
            b.op("tensor", lambda e: e.matmul(
                psB[:, 0:8], P["ones1"].ap(), r2tmp.ap(), start=True, stop=True),
                reads=[r2tmp.k(), P["ones1"].k()], writes=[("psB", 0)],
                tag="mm_rb2")
            b.op("vector", lambda e: e.tensor_copy(P["rb2bc"].ap(), psB[:, 0:8]),
                 reads=[("psB", 0)], writes=[P["rb2bc"].k()], tag="ev_rb2")

            b.dma("qgp", lambda e: e.dma_start(
                out=P["rw2"].ap(),
                in_=di["L0_ffn_rw2"].rearrange("(k p) n -> p k n", p=128)),
                writes=[P["rw2"].k()], tag="ld_rw2")

            wxtmp = mem.alloc("wxtmp", [96, 2048], BF16)
            b.dma("qgp", lambda e: e.dma_start(
                out=wxtmp.ap(), in_=di["L0_attn_a_W_xp"]),
                writes=[wxtmp.k()], tag="ld_wxp")
            for t in range(16):
                def trx(e, t=t):
                    return e.transpose(pT(t)[:, 0:96],
                                       wxtmp.ap()[0:96, t * 128:(t + 1) * 128],
                                       P["idtb"].ap()[0:96, 0:96])
                b.op("tensor", trx, reads=[wxtmp.k(), P["idtb"].k()],
                     writes=[pTk(t)], tag="tr_wxp")
                b.op("vector", lambda e, t=t: e.tensor_copy(
                    P["wxpT"].ap()[:, t, :], pT(t)[:, 0:96]),
                    reads=[pTk(t)], writes=[P["wxpT"].k()], tag="ev_wxp")

            wdtmp = mem.alloc("wdtmp", [128, 16, 64], BF16)
            b.dma("qgp", lambda e: e.dma_start(
                out=wdtmp.ap(),
                in_=di["L0_attn_a_W_dt"].rearrange("(t p) r -> p t r", p=128)),
                writes=[wdtmp.k()], tag="ld_wdt")
            for t in range(16):
                def trd(e, t=t):
                    return e.transpose(pT(t)[0:64, :], wdtmp.ap()[:, t, :],
                                       P["idtb"].ap())
                b.op("tensor", trd, reads=[wdtmp.k(), P["idtb"].k()],
                     writes=[pTk(t)], tag="tr_wdt")
                b.op("vector", lambda e, t=t: e.tensor_copy(
                    P["wdtT"].ap()[0:64, t, :], pT(t)[0:64, :]),
                    reads=[pTk(t)], writes=[P["wdtT"].k()], tag="ev_wdt")

            wbuf = mem.alloc("wbuf", [128, 8, 2048], BF16)

            def w_to_scratch(src, rows, cols, scr, tag):
                ctiles = cols // 128
                stg = _View(mem, "stg" + tag, mem.off, [128, ctiles, rows], BF16)
                stg_end = stg.off + ctiles * rows * 2
                assert stg_end <= mem.cap
                mem.peak = max(mem.peak, stg_end)
                for pc in range(rows // 1024):
                    def ldw(e, pc=pc):
                        return e.dma_start(
                            out=wbuf.ap()[:, :, 0:cols],
                            in_=src[pc * 1024:(pc + 1) * 1024, :]
                            .rearrange("(et p) d -> p et d", p=128))
                    b.dma("qgp", ldw, writes=[wbuf.k()], tag="ldw" + tag)
                    for et in range(8):
                        for dt in range(ctiles):
                            slot = (et * ctiles + dt) % 8
                            trbf(wbuf.ap()[:, et, dt * 128:(dt + 1) * 128],
                                 [wbuf.k()],
                                 stg.ap()[:, dt, (pc * 8 + et) * 128:
                                          (pc * 8 + et) * 128 + 128],
                                 [stg.k(dt), ("stgreg",)], slot, tag)
                def stw(e):
                    return e.dma_start(out=scr.rearrange("dt p e -> p dt e"),
                                      in_=stg.ap())
                b.dma("qsync", stw, reads=[stg.k(dt) for dt in range(ctiles)],
                      writes=[("stgreg",)], tag="stw" + tag)

            msub = mem.mark()
            w_to_scratch(di["L0_attn_a_W_in"], 2 * DI, DM, winT_scr, "win")
            mem.release(msub)
            w_to_scratch(di["L0_attn_a_W_out"], DM, DI, woutT_scr, "wout")
            mem.release(msub)
            for w, nm in (("q", "L0_attn_b_W_q"), ("k", "L0_attn_b_W_k"),
                          ("v", "L0_attn_b_W_v"), ("o", "L0_attn_b_W_o")):
                w_to_scratch(di[nm], DM, DM, wT_scr[w], "w" + w)
                mem.release(msub)

            b.op("vector", lambda e: e.memset(P["hst"].ap(), 0.0),
                 writes=[P["hst"].k()] + [P["hst"].k(c) for c in range(256)],
                 tag="init_hst")
            mem.release(m0)

        def mamba():
            m0 = mem.mark()
            mx = mem.alloc("mx", [128, 4, 1024], F32)
            mxb = mem.alias(mx, [128, 4, 1024], BF16)
            my0 = mem.alloc("my0", [128, 4, 1024], F32)
            y2bf = mem.alias(my0, [128, 16, 512], BF16)
            st1 = mem.alloc("st1", [128, 4, 4], F32)
            st2 = mem.alloc("st2", [128, 4, 4], F32)
            sdump = mem.alloc("sdump", [128, 1024], F32)
            mxnt = mem.alloc("mxnt", [128, 8, 512], BF16)
            xastg = mem.alloc("xastg", [128, 1024], F32)
            wostg = mem.alloc("wostg", [128, 2, 512], F32)
            mu = mem.alloc("mu", [128, 16, 515], F32)
            muc16 = mem.alloc("mc16", [128, 16, 512], BF16)
            zbuf = mem.alias(muc16, [128, 8, 512], F32)
            ucstg = mem.alloc("ucstg", [128, 2, 512], F32)
            ucdg = mem.alloc("ucdg", [128, 4, 512], F32)
            mwin = mem.alloc("mwin", [128, 2, 8, 128], BF16)
            mwout = mem.alloc("mwout", [128, 1, 16, 256], BF16)
            mdtl = mem.alloc("mdtl", [64, 512], BF16)
            mbcs = mem.alloc("mbcs", [16, 2, 512], F32)
            mdtu = mem.alloc("mdtu", [128, 4, 512], F32)
            scB = mem.alloc("scB", [128, 2, 512], F32)
            scC = mem.alloc("scC", [128, 2, 512], F32)
            dA2 = mem.alloc("dA2", [128, 3, 512], F32)
            dBu2 = mem.alloc("dBu2", [128, 3, 512], F32)
            hs2 = mem.alloc("hs2", [128, 3, 512], F32)
            pr2 = mem.alloc("pr2", [128, 3, 512], BF16)
            spt = mem.alloc("spt", [128, 512], F32)
            spr = mem.alloc("spr", [128, 512], F32)
            ysb = mem.alloc("ysb", [128, 512], F32)
            zst = mem.alloc("zst", [128, 1, 512], F32)

            b.op("vector", lambda e: e.memset(mu.ap()[:, :, 0:3], 0.0),
                 writes=[("mu", "h", t) for t in range(16)], tag="init_halo")

            for c in range(MCHUNKS):
                def ldx(e, c=c):
                    return e.dma_start(
                        out=mx.ap(),
                        in_=di["L0_attn_a_x"][c * CH:(c + 1) * CH, :]
                        .rearrange("(mt p) d -> p mt d", p=128))
                b.dma("qsync", ldx, writes=[mx.k()], tag=f"m_ldx{c}")

                for mt in range(4):
                    negmu, rstd, kng, krs = ln_stats(
                        mx.ap()[:, mt, :], [mx.k()], st1, mt, sdump, f"a{c}{mt}")
                    def y0f(e, mt=mt, negmu=negmu, rstd=rstd):
                        return e.tensor_scalar(
                            my0.ap()[:, mt, :], mx.ap()[:, mt, :],
                            negmu, rstd, ALU.add, ALU.mult)
                    b.op("vector", y0f, reads=[mx.k()], sreads=[kng, krs],
                         writes=[my0.k()], tag=f"y0_{c}{mt}")
                    def xaf(e, mt=mt):
                        return e.tensor_add(xastg.ap(), mx.ap()[:, mt, :],
                                            my0.ap()[:, mt, :])
                    b.op("vector", xaf, reads=[mx.k(), my0.k()],
                         writes=[xastg.k()], tag=f"xa_{c}{mt}")
                    def stxa(e, c=c, mt=mt):
                        return e.dma_start(
                            out=x_state[c * CH + mt * 128: c * CH + (mt + 1) * 128, :],
                            in_=xastg.ap())
                    b.dma("qsync", stxa, reads=[xastg.k()],
                          writes=[("xst", c, mt)], tag=f"stxa{c}{mt}")
                    negmu2, rstd2, kng2, krs2 = ln_stats(
                        my0.ap()[:, mt, :], [my0.k()], st2, mt, sdump,
                        f"b{c}{mt}")
                    def t1f(e, mt=mt, negmu2=negmu2, rstd2=rstd2):
                        return e.tensor_scalar(
                            mxb.ap()[:, mt, :], my0.ap()[:, mt, :],
                            negmu2, rstd2, ALU.add, ALU.mult)
                    b.op("vector", t1f, reads=[my0.k()], sreads=[kng2, krs2],
                         writes=[mx.k()], tag=f"t1_{c}{mt}")

                for mt in range(4):
                    for dk in range(8):
                        slot = (mt * 8 + dk) % 8
                        b.op("tensor", lambda e, mt=mt, dk=dk, slot=slot:
                             e.transpose(pT(slot),
                                         mxb.ap()[:, mt, dk * 128:(dk + 1) * 128],
                                         P["idtb"].ap()),
                             reads=[mx.k(), P["idtb"].k()], writes=[pTk(slot)],
                             tag=f"trm{c}")
                        def gap(e, mt=mt, dk=dk, slot=slot):
                            return e.tensor_scalar(
                                mxnt.ap()[:, dk, mt * 128:(mt + 1) * 128],
                                pT(slot), P["lng"].ap()[:, dk:dk + 1],
                                P["lnb"].ap()[:, dk:dk + 1], ALU.mult, ALU.add)
                        b.op("vector", gap, reads=[pTk(slot)],
                             sreads=[P["lng"].k(), P["lnb"].k()],
                             writes=[mxnt.k(dk)], tag=f"gam{c}")

                if MSTAGE < 2:
                    continue
                for g in range(32):
                    par = g % 2
                    def ldwin(e, g=g, par=par):
                        return e.dma_start(
                            out=mwin.ap()[:, par],
                            in_=winT_scr[:, :, g * 128:(g + 1) * 128]
                            .rearrange("k p e -> p k e"))
                    b.dma("qsync", ldwin, writes=[mwin.k(par)], tag=f"lwin{c}{g}")
                    if True:
                        et = g
                        bank = et % 3
                        for kk in range(8):
                            def mmw(e, kk=kk, par=par, bank=bank):
                                return e.matmul(
                                    pA(bank),
                                    mwin.ap()[:, par, kk, :],
                                    mxnt.ap()[:, kk, :],
                                    start=(kk == 0), stop=(kk == 7))
                            b.op("tensor", mmw,
                                 reads=[mwin.k(par), mxnt.k(kk), ("psA", bank)],
                                 writes=[("psA", bank)], tag=f"mmwin{c}")
                        if et < 16:
                            def evu(e, et=et, bank=bank):
                                return e.copy(mu.ap()[:, et, 3:515], pA(bank))
                            b.op("scalar", evu, reads=[("psA", bank)],
                                 writes=[("mu", et)], tag=f"evu{c}")
                        else:
                            zt = et - 16
                            zp = 0
                            def evz(e, bank=bank, zp=zp):
                                return e.activation(zst.ap()[:, zp, :], pA(bank),
                                                    AF.Silu,
                                                    scale=P["cst"].ap()[:, 3:4])
                            b.op("scalar", evz, reads=[("psA", bank)],
                                 sreads=[P["cst"].k()],
                                 writes=[zst.k(zp)], tag=f"evz{c}")
                            def stz(e, c=c, zt=zt, zp=zp):
                                return e.dma_start(out=z_scr[c, zt],
                                                   in_=zst.ap()[:, zp, :])
                            b.dma("qsca", stz, reads=[zst.k(zp)], tag=f"stz{c}")

                if MSTAGE < 3:
                    continue
                for t in range(16):
                    def cv0(e, t=t):
                        return e.tensor_scalar_mul(
                            spr.ap(), mu.ap()[:, t, 0:512],
                            P["convw"].ap()[:, t, 0:1])
                    b.op("vector", cv0, reads=[("mu", t), ("mu", "h", t)],
                         sreads=[P["convw"].k()], writes=[spr.k()], tag=f"cv{c}")
                    for kk in range(1, 4):
                        def cvk(e, t=t, kk=kk):
                            return e.scalar_tensor_tensor(
                                spr.ap(), mu.ap()[:, t, kk:kk + 512],
                                P["convw"].ap()[:, t, kk:kk + 1],
                                spr.ap(), ALU.mult, ALU.add)
                        b.op("vector", cvk,
                             reads=[("mu", t), ("mu", "h", t), spr.k()],
                             sreads=[P["convw"].k()], writes=[spr.k()],
                             tag=f"cv{c}")
                    def hcp(e, t=t):
                        return e.tensor_copy(mu.ap()[:, t, 0:3],
                                             mu.ap()[:, t, 512:515])
                    b.op("vector", hcp, reads=[("mu", t)],
                         writes=[("mu", "h", t)], tag=f"hcp{c}")
                    up = t % 2
                    def sl32(e, t=t, up=up):
                        return e.activation(ucstg.ap()[:, up, :], spr.ap(),
                                            AF.Silu,
                                            bias=P["convb"].ap()[:, t:t + 1])
                    b.op("scalar", sl32, reads=[spr.k()],
                         sreads=[P["convb"].k()], writes=[ucstg.k(up)],
                         tag=f"sl{c}")
                    def stuc(e, c=c, t=t, up=up):
                        return e.dma_start(out=uc_scr[c, t],
                                           in_=ucstg.ap()[:, up, :])
                    b.dma("qsca", stuc, reads=[ucstg.k(up)],
                          writes=[("ucs", c, t)], tag=f"stuc{c}")
                    def slb(e, t=t, up=up):
                        return e.tensor_copy(muc16.ap()[:, t, :],
                                             ucstg.ap()[:, up, :])
                    b.op("vector", slb, reads=[ucstg.k(up)],
                         writes=[("mc16", t)], tag=f"slb{c}")

                if MSTAGE < 4:
                    continue
                for kk in range(16):
                    def mmx1(e, kk=kk):
                        return e.matmul(psB[0:64, 0:512],
                                        P["wxpT"].ap()[:, kk, 0:64],
                                        muc16.ap()[:, kk, :],
                                        start=(kk == 0), stop=(kk == 15))
                    b.op("tensor", mmx1,
                         reads=[P["wxpT"].k(), ("mc16", kk), ("psB", 0)],
                         writes=[("psB", 0)], tag=f"mmxa{c}")
                for kk in range(16):
                    def mmx2(e, kk=kk):
                        return e.matmul(psB[0:16, 512:1024],
                                        P["wxpT"].ap()[:, kk, 64:80],
                                        muc16.ap()[:, kk, :],
                                        start=(kk == 0), stop=(kk == 15))
                    b.op("tensor", mmx2,
                         reads=[P["wxpT"].k(), ("mc16", kk), ("psB", 1)],
                         writes=[("psB", 1)], tag=f"mmxb{c}")
                for kk in range(16):
                    def mmx3(e, kk=kk):
                        return e.matmul(pA(0)[0:16, :],
                                        P["wxpT"].ap()[:, kk, 80:96],
                                        muc16.ap()[:, kk, :],
                                        start=(kk == 0), stop=(kk == 15))
                    b.op("tensor", mmx3,
                         reads=[P["wxpT"].k(), ("mc16", kk), ("psA", 0)],
                         writes=[("psA", 0)], tag=f"mmxc{c}")
                b.op("scalar", lambda e: e.copy(mdtl.ap(), psB[0:64, 0:512]),
                     reads=[("psB", 0)], writes=[mdtl.k()], tag=f"edtl{c}")
                b.op("vector", lambda e: e.tensor_copy(
                    mbcs.ap()[:, 0, :], psB[0:16, 512:1024]),
                    reads=[("psB", 1)], writes=[mbcs.k("B")], tag=f"evB{c}")
                b.op("vector", lambda e: e.tensor_copy(
                    mbcs.ap()[:, 1, :], pA(0)[0:16, :]),
                    reads=[("psA", 0)], writes=[mbcs.k("C")], tag=f"evC{c}")

                if MSTAGE < 5:
                    continue
                for t in range(16):
                    bank = 1 + (t % 2)
                    def mmdt(e, t=t, bank=bank):
                        return e.matmul(pA(bank), P["wdtT"].ap()[:, t, :],
                                        mdtl.ap(), start=True, stop=True)
                    b.op("tensor", mmdt,
                         reads=[P["wdtT"].k(), mdtl.k(), ("psA", bank)],
                         writes=[("psA", bank)], tag=f"mmdt{c}")
                    if MSTAGE == 41:
                        def spc(e, bank=bank):
                            return e.tensor_copy(spr.ap(), pA(bank))
                        b.op("vector", spc, reads=[("psA", bank)],
                             writes=[spr.k()], tag=f"spc{c}")
                        continue
                    def sp1(e, t=t, bank=bank):
                        return e.tensor_scalar(
                            spr.ap(), pA(bank), P["bdt"].ap()[:, t:t + 1], 0.0,
                            ALU.add, ALU.max)
                    b.op("vector", sp1, reads=[("psA", bank), spr.k()],
                         sreads=[P["bdt"].k()], writes=[spr.k()], tag=f"sp1{c}")
                    if MSTAGE == 42:
                        continue
                    def sp2(e, t=t, bank=bank):
                        return e.tensor_scalar(
                            spt.ap(), pA(bank), P["bdt"].ap()[:, t:t + 1], 0.0,
                            ALU.add, ALU.min)
                    b.op("vector", sp2, reads=[("psA", bank), spt.k()],
                         sreads=[P["bdt"].k()], writes=[spt.k()], tag=f"sp2{c}")
                    def sp2b(e):
                        return e.tensor_sub(spt.ap(), spr.ap(), spt.ap())
                    b.op("vector", sp2b, reads=[spr.k(), spt.k()],
                         writes=[spt.k()], tag=f"sp2b{c}")
                    b.op("scalar", lambda e: e.activation(
                        spt.ap(), spt.ap(), AF.Exp, scale=-1.0),
                        reads=[spt.k()], writes=[spt.k()], tag=f"sp3{c}")
                    if MSTAGE == 43:
                        continue
                    b.op("scalar", lambda e: e.activation(
                        spt.ap(), spt.ap(), AF.Ln, bias=1.0),
                        reads=[spt.k()], writes=[spt.k()], tag=f"sp4{c}")
                    def sp5(e, t=t):
                        return e.tensor_add(mu.ap()[:, t, 3:515], spr.ap(),
                                            spt.ap())
                    b.op("vector", sp5, reads=[spr.k(), spt.k()],
                         writes=[("mu", t)], tag=f"sp5{c}")

                if MSTAGE < 6 or MSTAGE in (41, 42, 43):
                    continue
                def ldz(e, c=c):
                    return e.dma_start(
                        out=zbuf.ap(),
                        in_=z_scr[c, 0:8].rearrange("zt p t -> p zt t"))
                b.dma("qsync", ldz, writes=[("mc16", t) for t in range(16)],
                      tag=f"ldz{c}")

                it = 0
                for dg in range(4):
                    def lduc(e, c=c, dg=dg):
                        return e.dma_start(
                            out=ucdg.ap(),
                            in_=uc_scr[c, dg * 4:(dg + 1) * 4]
                            .rearrange("t p x -> p t x"))
                    b.dma("qsync", lduc,
                          reads=[("ucs", c, dg * 4 + j) for j in range(4)],
                          writes=[ucdg.k()], tag=f"lduc{c}{dg}")
                    for dj in range(4):
                        t = dg * 4 + dj
                        def dtu(e, t=t, dj=dj):
                            return e.tensor_mul(mdtu.ap()[:, dj, :],
                                                mu.ap()[:, t, 3:515],
                                                ucdg.ap()[:, dj, :])
                        b.op("vector", dtu, reads=[("mu", t), ucdg.k()],
                             writes=[mdtu.k(dj)], tag=f"dtu{c}")
                    for n in range(16):
                        pn = n % 2
                        def bcB(e, n=n):
                            return e.matmul(pB(0),
                                            P["oneh"].ap()[:, n * 128:(n + 1) * 128],
                                            mbcs.ap()[:, 0, :],
                                            start=True, stop=True)
                        b.op("tensor", bcB,
                             reads=[P["oneh"].k(), mbcs.k("B"), ("psB", 0)],
                             writes=[("psB", 0)], tag=f"bcB{c}")
                        def bcC(e, n=n):
                            return e.matmul(pB(1),
                                            P["oneh"].ap()[:, n * 128:(n + 1) * 128],
                                            mbcs.ap()[:, 1, :],
                                            start=True, stop=True)
                        b.op("tensor", bcC,
                             reads=[P["oneh"].k(), mbcs.k("C"), ("psB", 1)],
                             writes=[("psB", 1)], tag=f"bcC{c}")
                        b.op("scalar", lambda e, pn=pn: e.copy(
                            scB.ap()[:, pn, :], pB(0)),
                            reads=[("psB", 0)], writes=[scB.k(pn)], tag=f"cB{c}")
                        b.op("scalar", lambda e, pn=pn: e.copy(
                            scC.ap()[:, pn, :], pB(1)),
                            reads=[("psB", 1)], writes=[scC.k(pn)], tag=f"cC{c}")
                        for dj in range(4):
                            t = dg * 4 + dj
                            col = t * 16 + n
                            pi = it % 3
                            it += 1
                            def fdA(e, t=t, n=n, pi=pi):
                                return e.activation(
                                    dA2.ap()[:, pi, :], mu.ap()[:, t, 3:515],
                                    AF.Exp, scale=P["A"].ap()[:, t, n:n + 1])
                            b.op("scalar", fdA, reads=[("mu", t)],
                                 sreads=[P["A"].k()], writes=[dA2.k(pi)],
                                 tag=f"dA{c}")
                            def fdBu(e, dj=dj, pn=pn, pi=pi):
                                return e.tensor_mul(dBu2.ap()[:, pi, :],
                                                    mdtu.ap()[:, dj, :],
                                                    scB.ap()[:, pn, :])
                            b.op("gpsimd", fdBu,
                                 reads=[mdtu.k(dj), scB.k(pn)],
                                 writes=[dBu2.k(pi)], tag=f"dBu{c}")
                            def fscan(e, col=col, pi=pi):
                                return e.tensor_tensor_scan(
                                    hs2.ap()[:, pi, :], dA2.ap()[:, pi, :],
                                    dBu2.ap()[:, pi, :],
                                    P["hst"].ap()[:, col:col + 1],
                                    ALU.mult, ALU.add)
                            b.op("vector", fscan,
                                 reads=[dA2.k(pi), dBu2.k(pi)],
                                 sreads=[P["hst"].k(col)],
                                 writes=[hs2.k(pi)], tag=f"scan{c}")
                            def fhs(e, col=col, pi=pi):
                                return e.tensor_copy(
                                    P["hst"].ap()[:, col:col + 1],
                                    hs2.ap()[:, pi, 511:512])
                            b.op("gpsimd", fhs, reads=[hs2.k(pi)],
                                 writes=[P["hst"].k(col)], tag=f"hs{c}")
                            def fpr(e, pn=pn, pi=pi):
                                return e.tensor_mul(pr2.ap()[:, pi, :],
                                                    hs2.ap()[:, pi, :],
                                                    scC.ap()[:, pn, :])
                            b.op("vector", fpr,
                                 reads=[hs2.k(pi), scC.k(pn)],
                                 writes=[pr2.k(pi)], tag=f"pr{c}")
                            def fyac(e, dj=dj, n=n, pi=pi):
                                return e.matmul(pA(dj), P["idtb"].ap(),
                                                pr2.ap()[:, pi, :],
                                                start=(n == 0), stop=(n == 15))
                            b.op("tensor", fyac,
                                 reads=[pr2.k(pi), P["idtb"].k(), ("psA", dj)],
                                 writes=[("psA", dj)], tag=f"yac{c}")
                    for dj in range(4):
                        t = dg * 4 + dj
                        def yev(e, t=t, dj=dj):
                            return e.scalar_tensor_tensor(
                                ysb.ap(), ucdg.ap()[:, dj, :],
                                P["Dv"].ap()[:, t:t + 1], pA(dj),
                                ALU.mult, ALU.add)
                        b.op("vector", yev,
                             reads=[ucdg.k(), ("psA", dj), ysb.k()],
                             sreads=[P["Dv"].k()], writes=[ysb.k()],
                             tag=f"yev{c}")
                        def y2(e, t=t):
                            return e.tensor_mul(y2bf.ap()[:, t, :], ysb.ap(),
                                                zbuf.ap()[:, t % 8, :])
                        b.op("vector", y2,
                             reads=[ysb.k(), ("mc16", 0), my0.k()],
                             writes=[my0.k()], tag=f"y2{c}")
                    if dg == 1:
                        def ldz2(e, c=c):
                            return e.dma_start(
                                out=zbuf.ap(),
                                in_=z_scr[c, 8:16].rearrange("zt p t -> p zt t"))
                        b.dma("qsync", ldz2,
                              writes=[("mc16", t) for t in range(16)],
                              tag=f"ldz2{c}")

                if MSTAGE < 7 or MSTAGE in (41, 42, 43):
                    continue
                for piece in range(4):
                    wp = 0
                    def ldwo(e, piece=piece, wp=wp):
                        return e.dma_start(
                            out=mwout.ap()[:, wp],
                            in_=woutT_scr[:, :, piece * 256:(piece + 1) * 256]
                            .rearrange("k p d -> p k d"))
                    b.dma("qsync", ldwo, writes=[("mwout", wp)],
                          tag=f"lwo{c}{piece}")
                    for mt in range(4):
                        bank = mt % 2
                        for kk in range(16):
                            def mmo(e, mt=mt, kk=kk, wp=wp, bank=bank):
                                return e.matmul(
                                    pB(bank)[:, 0:256],
                                    y2bf.ap()[:, kk, mt * 128:(mt + 1) * 128],
                                    mwout.ap()[:, wp, kk, :],
                                    start=(kk == 0), stop=(kk == 15))
                            b.op("tensor", mmo,
                                 reads=[my0.k(), ("mwout", wp), ("psB", bank)],
                                 writes=[("psB", bank)], tag=f"mmo{c}")
                        sp = mt % 2
                        def oev(e, bank=bank, sp=sp):
                            return e.tensor_copy(wostg.ap()[:, sp, 0:256],
                                                 pB(bank)[:, 0:256])
                        b.op("vector", oev, reads=[("psB", bank)],
                             writes=[wostg.k(sp)], tag=f"oev{c}")
                        def oacc(e, c=c, mt=mt, piece=piece, sp=sp):
                            return e.dma_start(
                                out=x_state[c * CH + mt * 128:
                                            c * CH + (mt + 1) * 128,
                                            piece * 256:(piece + 1) * 256],
                                in_=wostg.ap()[:, sp, 0:256],
                                accum_op=ALU.add)
                        b.dma("qgp", oacc, reads=[wostg.k(sp)],
                              writes=[("xst", c, mt)], tag=f"oacc{c}")
            mem.release(m0)

        def ffn(call):
            m0 = mem.mark()
            rw1 = mem.alloc("rw1", [128, 8, 1024], BF16)
            b.dma("qgp", lambda e: e.dma_start(
                out=rw1.ap(),
                in_=di["L0_ffn_rw1"].rearrange("(k p) o -> p k o", p=128)),
                writes=[rw1.k()], tag=f"ld_rw1_{call}")
            fx = mem.alloc("fx", [128, 8, 1024], F32)
            g1t = mem.alias(fx, [128, 8, 1024], BF16)
            fxacc = mem.alloc("fxacc", [128, 8, 1024], F32)
            fxnb = mem.alloc("fxnb", [128, 1024], BF16)
            fxnt = mem.alloc("fxnt", [128, 8, 1024], BF16)
            we1b = mem.alloc("we1b", [128, 2, 8, 1024], BF16)
            we2b = mem.alloc("we2b", [128, 2, 8, 1024], BF16)
            hb = mem.alloc("hb", [128, 8, 512], BF16)
            xn32 = mem.alloc("xn32", [128, 1024], F32)
            wnat = mem.alloc("wnat", [128, 8, 8], F32)
            fwT = mem.alloc("fwT", [8, 1024], F32)
            lgv = mem.alloc("lgv", [128, 8], F32)
            tk1 = mem.alloc("tk1", [128, 8], F32)
            tk2 = mem.alloc("tk2", [128, 4], F32)
            stf = mem.alloc("stf", [128, 8, 4], F32)
            sdump = mem.alias(xn32, [128, 1024], F32)

            first_conv = (call == 0)
            for q in range(4):
                q0 = q * 1024
                def ldx(e, q0=q0):
                    return e.dma_start(
                        out=fx.ap(),
                        in_=x_state[q0:q0 + 1024, :]
                        .rearrange("(mt p) d -> p mt d", p=128))
                b.dma("qsync", ldx,
                      reads=[("xst", (q0 // CH) + i, mt) for i in range(2)
                             for mt in range(4)],
                      writes=[fx.k()], tag=f"f{call}ldx{q}")

                for mt in range(8):
                    negmu, rstd, kng, krs = ln_stats(
                        fx.ap()[:, mt, :], [fx.k()], stf, mt, sdump,
                        f"f{call}{q}{mt}")
                    def xnf(e, mt=mt, negmu=negmu, rstd=rstd):
                        return e.tensor_scalar(
                            xn32.ap(), fx.ap()[:, mt, :], negmu, rstd,
                            ALU.add, ALU.mult)
                    b.op("vector", xnf, reads=[fx.k()], sreads=[kng, krs],
                         writes=[xn32.k()], tag=f"fxn{call}{q}{mt}")
                    b.op("vector", lambda e, mt=mt: e.tensor_add(
                        fxacc.ap()[:, mt, :], fx.ap()[:, mt, :], xn32.ap()),
                        reads=[fx.k(), xn32.k()], writes=[fxacc.k(mt)],
                        tag=f"fxa{call}{q}{mt}")
                    b.op("vector", lambda e, mt=mt: e.tensor_copy(
                        fxnb.ap(), xn32.ap()),
                        reads=[xn32.k()], writes=[fxnb.k()],
                        tag=f"fnb{call}{q}{mt}")
                    for dk in range(8):
                        slot = (mt * 8 + dk) % 8
                        b.op("tensor", lambda e, dk=dk, slot=slot:
                             e.transpose(pT(slot),
                                         fxnb.ap()[:, dk * 128:(dk + 1) * 128],
                                         P["idtb"].ap()),
                             reads=[fxnb.k(), P["idtb"].k()],
                             writes=[pTk(slot)], tag=f"ftr{call}")
                        b.op("vector", lambda e, mt=mt, dk=dk, slot=slot:
                             e.tensor_copy(
                                 fxnt.ap()[:, dk, mt * 128:(mt + 1) * 128],
                                 pT(slot)),
                             reads=[pTk(slot)], writes=[fxnt.k(dk)],
                             tag=f"fte{call}")

                for ot in range(8):
                    for th in range(2):
                        bank = (ot * 2 + th) % 3
                        for kk in range(8):
                            def mg(e, ot=ot, th=th, kk=kk, bank=bank):
                                return e.matmul(
                                    pA(bank),
                                    rw1.ap()[:, kk, ot * 128:(ot + 1) * 128],
                                    fxnt.ap()[:, kk, th * 512:(th + 1) * 512],
                                    start=(kk == 0), stop=(kk == 7))
                            b.op("tensor", mg,
                                 reads=[rw1.k(), fxnt.k(kk), ("psA", bank)],
                                 writes=[("psA", bank)], tag=f"fg1{call}")
                        def eg(e, ot=ot, th=th, bank=bank):
                            return e.activation(
                                g1t.ap()[:, ot, th * 512:(th + 1) * 512],
                                pA(bank), AF.Silu,
                                bias=P["rb1T"].ap()[:, ot:ot + 1])
                        b.op("scalar", eg, reads=[("psA", bank)],
                             sreads=[P["rb1T"].k()], writes=[fx.k()],
                             tag=f"fge{call}")
                for mt in range(8):
                    for kk in range(8):
                        def ml(e, mt=mt, kk=kk):
                            return e.matmul(
                                psB[:, 0:8],
                                g1t.ap()[:, kk, mt * 128:(mt + 1) * 128],
                                P["rw2"].ap()[:, kk, :],
                                start=(kk == 0), stop=(kk == 7))
                        b.op("tensor", ml,
                             reads=[fx.k(), P["rw2"].k(), ("psB", 0)],
                             writes=[("psB", 0)], tag=f"flg{call}")
                    b.op("vector", lambda e: e.tensor_add(
                        lgv.ap(), psB[:, 0:8], P["rb2bc"].ap()),
                        reads=[("psB", 0), P["rb2bc"].k()], writes=[lgv.k()],
                        tag=f"flgv{call}")
                    b.op("vector", lambda e: e.reduce_max(
                        tk2.ap()[:, 0:1], lgv.ap(), AX.X),
                        reads=[lgv.k()], writes=[("tk2", 0)], tag="t2a")
                    b.op("vector", lambda e: e.tensor_scalar(
                        tk1.ap(), lgv.ap(), tk2.ap()[:, 0:1], None,
                        ALU.is_equal),
                        reads=[lgv.k()], sreads=[("tk2", 0)], writes=[tk1.k()],
                        tag="t2b")
                    b.op("vector", lambda e: e.tensor_scalar_mul(
                        tk1.ap(), tk1.ap(), -1e30),
                        reads=[tk1.k()], writes=[tk1.k()], tag="t2c")
                    b.op("vector", lambda e: e.tensor_add(
                        tk1.ap(), tk1.ap(), lgv.ap()),
                        reads=[tk1.k(), lgv.k()], writes=[tk1.k()], tag="t2d")
                    b.op("vector", lambda e: e.reduce_max(
                        tk2.ap()[:, 1:2], tk1.ap(), AX.X),
                        reads=[tk1.k()], writes=[("tk2", 1)], tag="t2e")
                    b.op("vector", lambda e: e.tensor_scalar(
                        tk1.ap(), lgv.ap(), tk2.ap()[:, 1:2], None, ALU.is_ge),
                        reads=[lgv.k()], sreads=[("tk2", 1)], writes=[tk1.k()],
                        tag="t2f")
                    b.op("vector", lambda e: e.reduce_max(
                        tk2.ap()[:, 2:3], lgv.ap(), AX.X, negate=True),
                        reads=[lgv.k()], writes=[("tk2", 2)], tag="t2g")
                    def t2h(e, mt=mt):
                        return e.activation(wnat.ap()[:, mt, :], lgv.ap(),
                                            AF.Exp, bias=tk2.ap()[:, 2:3])
                    b.op("scalar", t2h, reads=[lgv.k()], sreads=[("tk2", 2)],
                         writes=[wnat.k(mt)], tag="t2h")
                    def t2i(e, mt=mt):
                        return e.tensor_mul(wnat.ap()[:, mt, :],
                                            wnat.ap()[:, mt, :], tk1.ap())
                    b.op("vector", t2i, reads=[wnat.k(mt), tk1.k()],
                         writes=[wnat.k(mt)], tag="t2i")
                    def t2j(e, mt=mt):
                        return e.reduce_sum(tk2.ap()[:, 3:4],
                                            wnat.ap()[:, mt, :], AX.X)
                    b.op("vector", t2j, reads=[wnat.k(mt)], writes=[("tk2", 3)],
                         tag="t2j")
                    b.op("vector", lambda e: e.reciprocal(
                        tk2.ap()[:, 3:4], tk2.ap()[:, 3:4]),
                        reads=[("tk2", 3)], writes=[("tk2", 3)], tag="t2k")
                    def t2l(e, mt=mt):
                        return e.tensor_scalar(wnat.ap()[:, mt, :],
                                               wnat.ap()[:, mt, :],
                                               tk2.ap()[:, 3:4], None, ALU.mult)
                    b.op("vector", t2l, reads=[wnat.k(mt)], sreads=[("tk2", 3)],
                         writes=[wnat.k(mt)], tag="t2l")
                    def t2m(e, mt=mt):
                        return e.transpose(psB[0:8, 512:640],
                                           wnat.ap()[:, mt, :], P["idtf"].ap())
                    b.op("tensor", t2m, reads=[wnat.k(mt), P["idtf"].k(),
                                               ("psB", 1)],
                         writes=[("psB", 1)], tag="t2m")
                    def t2n(e, mt=mt):
                        return e.tensor_copy(
                            fwT.ap()[:, mt * 128:(mt + 1) * 128],
                            psB[0:8, 512:640])
                    b.op("vector", t2n, reads=[("psB", 1)], writes=[fwT.k()],
                         tag="t2n")

                for mt in range(8):
                    for nd in range(2):
                        def mbe(e, mt=mt, nd=nd):
                            return e.matmul(
                                pA(3)[:, 0:512],
                                fwT.ap()[:, mt * 128:(mt + 1) * 128],
                                P["be2n"].ap()[:, nd * 512:(nd + 1) * 512],
                                start=True, stop=True)
                        b.op("tensor", mbe,
                             reads=[fwT.k(), P["be2n"].k(), ("psA", 3)],
                             writes=[("psA", 3)], tag=f"fbe{call}")
                        def abe(e, mt=mt, nd=nd):
                            return e.tensor_add(
                                fxacc.ap()[:, mt, nd * 512:(nd + 1) * 512],
                                fxacc.ap()[:, mt, nd * 512:(nd + 1) * 512],
                                pA(3)[:, 0:512])
                        b.op("vector", abe, reads=[("psA", 3), fxacc.k(mt)],
                             writes=[fxacc.k(mt)], tag=f"fab{call}")

                for ee in range(NE):
                    for sl in range(4):
                        par = (ee * 4 + sl) % 2
                        if first_conv and q == 0:
                            def l1(e, ee=ee, sl=sl, par=par):
                                return e.dma_start(
                                    out=we1b.ap()[:, par],
                                    in_=di["L0_ffn_we1"][ee][:, sl * 1024:
                                                            (sl + 1) * 1024]
                                    .rearrange("(k p) h -> p k h", p=128))
                            b.dma("qgp", l1, writes=[("we1b", par)],
                                  tag=f"fl1{call}{q}")
                            def s1(e, ee=ee, sl=sl, par=par):
                                return e.dma_start(
                                    out=we1_bf[ee, sl],
                                    in_=we1b.ap()[:, par]
                                    .rearrange("p k h -> p (k h)"))
                            b.dma("qsca", s1, reads=[("we1b", par)],
                                  writes=[("w1s", ee, sl)], tag=f"fs1{call}")
                            def l2(e, ee=ee, sl=sl, par=par):
                                return e.dma_start(
                                    out=we2b.ap()[:, par],
                                    in_=di["L0_ffn_we2"][ee][sl * 1024:
                                                             (sl + 1) * 1024, :]
                                    .rearrange("(k p) d -> p k d", p=128))
                            b.dma("qgp", l2, writes=[("we2b", par)],
                                  tag=f"fl2{call}{q}")
                            def s2(e, ee=ee, sl=sl, par=par):
                                return e.dma_start(
                                    out=we2_bf[ee, sl],
                                    in_=we2b.ap()[:, par]
                                    .rearrange("p k h -> p (k h)"))
                            b.dma("qsca", s2, reads=[("we2b", par)],
                                  writes=[("w2s", ee, sl)], tag=f"fs2{call}")
                        else:
                            def l1(e, ee=ee, sl=sl, par=par):
                                return e.dma_start(
                                    out=we1b.ap()[:, par]
                                    .rearrange("p k h -> p (k h)"),
                                    in_=we1_bf[ee, sl])
                            b.dma("qsync", l1, reads=[("w1s", ee, sl)],
                                  writes=[("we1b", par)], tag=f"fl1{call}{q}")
                            def l2(e, ee=ee, sl=sl, par=par):
                                return e.dma_start(
                                    out=we2b.ap()[:, par]
                                    .rearrange("p k h -> p (k h)"),
                                    in_=we2_bf[ee, sl])
                            b.dma("qsca", l2, reads=[("w2s", ee, sl)],
                                  writes=[("we2b", par)], tag=f"fl2{call}{q}")
                        for c2 in range(2):
                            for ht in range(8):
                                bank = ht % 3
                                for kk in range(8):
                                    def m1(e, ht=ht, kk=kk, c2=c2, par=par,
                                           bank=bank):
                                        return e.matmul(
                                            pA(bank),
                                            we1b.ap()[:, par, kk,
                                                      ht * 128:(ht + 1) * 128],
                                            fxnt.ap()[:, kk, c2 * 512:
                                                      (c2 + 1) * 512],
                                            start=(kk == 0), stop=(kk == 7))
                                    b.op("tensor", m1,
                                         reads=[("we1b", par), fxnt.k(kk),
                                                ("psA", bank)],
                                         writes=[("psA", bank)],
                                         tag=f"fm1{call}")
                                def es(e, ht=ht, ee=ee, sl=sl, bank=bank):
                                    return e.activation(
                                        hb.ap()[:, ht, :], pA(bank), AF.Silu,
                                        bias=P["be1T"].ap()[:, ee * 32 + sl * 8
                                                            + ht:ee * 32
                                                            + sl * 8 + ht + 1])
                                b.op("scalar", es, reads=[("psA", bank)],
                                     sreads=[P["be1T"].k()],
                                     writes=[("hb", ht), ("hbh", ht // 4)],
                                     tag=f"fes{call}")
                            for mt2 in range(4):
                                mtg = c2 * 4 + mt2
                                for nd in range(2):
                                    bank = (mt2 * 2 + nd) % 2
                                    for kk in range(8):
                                        def m2(e, mt2=mt2, nd=nd, kk=kk,
                                               par=par, bank=bank):
                                            return e.matmul(
                                                pB(bank),
                                                hb.ap()[:, kk, mt2 * 128:
                                                        (mt2 + 1) * 128],
                                                we2b.ap()[:, par, kk,
                                                          nd * 512:
                                                          (nd + 1) * 512],
                                                start=(kk == 0),
                                                stop=(kk == 7))
                                        b.op("tensor", m2,
                                             reads=[("hbh", kk // 4),
                                                    ("we2b", par),
                                                    ("psB", bank)],
                                             writes=[("psB", bank)],
                                             tag=f"fm2{call}")
                                    def ev(e, mtg=mtg, nd=nd, ee=ee, bank=bank):
                                        return e.scalar_tensor_tensor(
                                            fxacc.ap()[:, mtg, nd * 512:
                                                       (nd + 1) * 512],
                                            pB(bank),
                                            wnat.ap()[:, mtg, ee:ee + 1],
                                            fxacc.ap()[:, mtg, nd * 512:
                                                       (nd + 1) * 512],
                                            ALU.mult, ALU.add)
                                    b.op("vector", ev,
                                         reads=[("psB", bank), fxacc.k(mtg)],
                                         sreads=[wnat.k(mtg)],
                                         writes=[fxacc.k(mtg)],
                                         tag=f"fev{call}")

                last = (call == 1)
                for mt in range(8):
                    def stq(e, q0=q0, mt=mt, last=last):
                        dst = out if last else x_state
                        return e.dma_start(
                            out=dst[q0 + mt * 128: q0 + (mt + 1) * 128, :],
                            in_=fxacc.ap()[:, mt, :])
                    b.dma("qsync", stq, reads=[fxacc.k(mt)],
                          writes=[("xst", (q0 + mt * 128) // CH,
                                   (mt % 4))] if not last else [],
                          tag=f"fst{call}{q}")
            mem.release(m0)

        def attn():
            m0 = mem.mark()
            vbuf = mem.alloc("vbuf", [128, 32, 16, 65], BF16)
            b.op("vector", lambda e: e.memset(
                vbuf.ap()[:, :, :, 64:65], 1.0),
                writes=[("vb1",)], tag="vones")

            m1m = mem.mark()
            awqk = mem.alloc("awqk", [128, 2, 8, 1024], BF16)
            axnt = mem.alloc("axnt", [128, 8, 512], BF16)
            ax = mem.alloc("ax", [128, 1024], F32)
            axn32 = mem.alloc("axn32", [128, 1024], F32)
            axnb = mem.alloc("axnb", [128, 1024], BF16)
            aq32 = mem.alloc("aq32", [128, 1024], F32)
            ak32 = mem.alloc("ak32", [128, 1024], F32)
            aro = mem.alloc("aro", [128, 1024], F32)
            arp = mem.alloc("arp", [128, 1024], F32)
            aqb = mem.alloc("aqb", [128, 1024], BF16)
            akb = mem.alloc("akb", [128, 1024], BF16)
            acos = mem.alloc("acos", [128, 512], F32)
            asin = mem.alloc("asin", [128, 512], F32)
            aqt = mem.alloc("aqt", [128, 8, 128], BF16)
            akt = mem.alloc("akt", [128, 8, 128], BF16)
            sta = mem.alloc("sta", [128, 1, 4], F32)
            sdump = mem.alloc("sdumpa", [128, 1024], F32)

            def rope(src, srck, dst, dstk, tag):
                sv = src.ap().rearrange("p (i two) -> p i two", two=2)
                dv = dst.ap().rearrange("p (i two) -> p i two", two=2)
                b.op("vector", lambda e: e.tensor_mul(
                    arp.ap()[:, 0:512], sv[:, :, 1], asin.ap()),
                    reads=[srck, asin.k()], writes=[("arp", 0)], tag="r1" + tag)
                b.op("vector", lambda e: e.tensor_mul(
                    arp.ap()[:, 512:1024], sv[:, :, 0], acos.ap()),
                    reads=[srck, acos.k()], writes=[("arp", 1)], tag="r2" + tag)
                b.op("vector", lambda e: e.tensor_sub(
                    dv[:, :, 0], arp.ap()[:, 512:1024], arp.ap()[:, 0:512]),
                    reads=[("arp", 0), ("arp", 1)], writes=[dstk], tag="r3" + tag)
                b.op("vector", lambda e: e.tensor_mul(
                    arp.ap()[:, 0:512], sv[:, :, 0], asin.ap()),
                    reads=[srck, asin.k()], writes=[("arp", 0)], tag="r4" + tag)
                b.op("vector", lambda e: e.tensor_mul(
                    arp.ap()[:, 512:1024], sv[:, :, 1], acos.ap()),
                    reads=[srck, acos.k()], writes=[("arp", 1)], tag="r5" + tag)
                b.op("vector", lambda e: e.tensor_add(
                    dv[:, :, 1], arp.ap()[:, 512:1024], arp.ap()[:, 0:512]),
                    reads=[("arp", 0), ("arp", 1), dstk], writes=[dstk],
                    tag="r6" + tag)

            for c in range(NCH):
                for wi, wn in ((0, "q"), (1, "k")):
                    def ldw(e, wi=wi, wn=wn):
                        return e.dma_start(
                            out=awqk.ap()[:, wi],
                            in_=wT_scr[wn].rearrange("k p d -> p k d"))
                    b.dma("qsca", ldw, writes=[("awqk", wi)], tag=f"aldw{c}{wn}")
                for tt in range(4):
                    gt = c * 4 + tt
                    def ldx(e, gt=gt):
                        return e.dma_start(
                            out=ax.ap(),
                            in_=x_state[gt * 128:(gt + 1) * 128, :])
                    b.dma("qsync", ldx,
                          reads=[("xst", gt // 4, gt % 4)],
                          writes=[ax.k()], tag=f"aldx{c}{tt}")
                    def ldcs(e, gt=gt):
                        return e.dma_start(out=acos.ap(),
                                           in_=cos_t[gt * 128:(gt + 1) * 128, :])
                    b.dma("qsync", ldcs, writes=[acos.k()], tag=f"acos{c}{tt}")
                    def ldsn(e, gt=gt):
                        return e.dma_start(out=asin.ap(),
                                           in_=sin_t[gt * 128:(gt + 1) * 128, :])
                    b.dma("qsync", ldsn, writes=[asin.k()], tag=f"asin{c}{tt}")

                    negmu, rstd, kng, krs = ln_stats(
                        ax.ap(), [ax.k()], sta, 0, sdump, f"at{c}{tt}")
                    b.op("vector", lambda e, negmu=negmu, rstd=rstd:
                         e.tensor_scalar(axn32.ap(), ax.ap(), negmu, rstd,
                                         ALU.add, ALU.mult),
                         reads=[ax.k()], sreads=[kng, krs], writes=[axn32.k()],
                         tag=f"axn{c}{tt}")
                    b.op("vector", lambda e: e.tensor_copy(axnb.ap(), axn32.ap()),
                         reads=[axn32.k()], writes=[axnb.k()], tag=f"axb{c}{tt}")
                    for dk in range(8):
                        slot = dk % 8
                        b.op("tensor", lambda e, dk=dk, slot=slot: e.transpose(
                            pT(slot), axnb.ap()[:, dk * 128:(dk + 1) * 128],
                            P["idtb"].ap()),
                            reads=[axnb.k(), P["idtb"].k()], writes=[pTk(slot)],
                            tag=f"atr{c}")
                        b.op("vector", lambda e, dk=dk, tt=tt, slot=slot:
                             e.tensor_copy(
                                 axnt.ap()[:, dk, tt * 128:(tt + 1) * 128],
                                 pT(slot)),
                             reads=[pTk(slot)], writes=[axnt.k(dk)],
                             tag=f"ate{c}")
                    for wi, dst, dstk in ((0, aq32, aq32.k()),
                                          (1, ak32, ak32.k())):
                        for nd in range(2):
                            bank = nd % 2
                            for kk in range(8):
                                def mq(e, wi=wi, nd=nd, kk=kk, tt=tt, bank=bank):
                                    return e.matmul(
                                        pA(bank),
                                        axnt.ap()[:, kk, tt * 128:(tt + 1) * 128],
                                        awqk.ap()[:, wi, kk,
                                                  nd * 512:(nd + 1) * 512],
                                        start=(kk == 0), stop=(kk == 7))
                                b.op("tensor", mq,
                                     reads=[axnt.k(kk), ("awqk", wi),
                                            ("psA", bank)],
                                     writes=[("psA", bank)], tag=f"amq{c}")
                            def eq_(e, dst=dst, nd=nd, bank=bank):
                                return e.tensor_copy(
                                    dst.ap()[:, nd * 512:(nd + 1) * 512],
                                    pA(bank))
                            b.op("vector", eq_, reads=[("psA", bank)],
                                 writes=[dstk], tag=f"aeq{c}")
                    rope(aq32, aq32.k(), aro, aro.k(), f"q{c}{tt}")
                    b.op("vector", lambda e: e.tensor_copy(aqb.ap(), aro.ap()),
                         reads=[aro.k()], writes=[aqb.k()], tag=f"aqb{c}")
                    rope(ak32, ak32.k(), aro, aro.k(), f"k{c}{tt}")
                    b.op("vector", lambda e: e.tensor_copy(akb.ap(), aro.ap()),
                         reads=[aro.k()], writes=[akb.k()], tag=f"akb{c}")
                    for src, dst, dstk, scr in ((aqb, aqt, "aqt", q_scr),
                                                (akb, akt, "akt", k_scr)):
                        for dj in range(8):
                            slot = dj % 8
                            b.op("tensor", lambda e, src=src, dj=dj, slot=slot:
                                 e.transpose(
                                     pT(slot),
                                     src.ap()[:, dj * 128:(dj + 1) * 128],
                                     P["idtb"].ap()),
                                 reads=[src.k(), P["idtb"].k()],
                                 writes=[pTk(slot)], tag=f"aqtr{c}")
                            b.op("vector", lambda e, dst=dst, dj=dj, slot=slot:
                                 e.tensor_copy(dst.ap()[:, dj, :], pT(slot)),
                                 reads=[pTk(slot)], writes=[(dstk,)],
                                 tag=f"aqte{c}")
                        def stq(e, scr=scr, c=c, tt=tt, dst=dst):
                            return e.dma_start(
                                out=scr[c, :, :, tt * 128:(tt + 1) * 128]
                                .rearrange("j p t -> p j t"),
                                in_=dst.ap())
                        b.dma("qsync", stq, reads=[(dstk,)],
                              writes=[("qk_scr", dstk, c)], tag=f"aqst{c}")

                def ldv(e):
                    return e.dma_start(
                        out=awqk.ap()[:, 0],
                        in_=wT_scr["v"].rearrange("k p d -> p k d"))
                b.dma("qsca", ldv, writes=[("awqk", 0)], tag=f"aldv{c}")
                for tt in range(4):
                    gt = c * 4 + tt
                    for nd in range(2):
                        bank = nd % 2
                        for kk in range(8):
                            def mv(e, nd=nd, kk=kk, tt=tt, bank=bank):
                                return e.matmul(
                                    pA(bank),
                                    axnt.ap()[:, kk, tt * 128:(tt + 1) * 128],
                                    awqk.ap()[:, 0, kk, nd * 512:(nd + 1) * 512],
                                    start=(kk == 0), stop=(kk == 7))
                            b.op("tensor", mv,
                                 reads=[axnt.k(kk), ("awqk", 0), ("psA", bank)],
                                 writes=[("psA", bank)], tag=f"amv{c}")
                        def evv(e, gt=gt, nd=nd, bank=bank):
                            return e.tensor_copy(
                                vbuf.ap()[:, gt, nd * 8:(nd + 1) * 8, 0:64],
                                pA(bank).rearrange("p (h d) -> p h d", d=64))
                        b.op("vector", evv, reads=[("psA", bank)],
                             writes=[("vb", gt)], tag=f"aev{c}")
            mem.release(m1m)
            b.barrier("attn_mid")

            awo = mem.alloc("awo", [128, 8, 1024], BF16)
            qtc = mem.alloc("qtc", [128, 8, 512], BF16)
            ktb = mem.alloc("ktb", [128, 4096], BF16)
            pbuf = mem.alloc("pbuf", [128, 32, 512], BF16)
            aoT = mem.alloc("aoT", [128, 8, 512], BF16)
            rc1 = mem.alloc("rc1", [1, 512], F32)
            rcb = mem.alloc("rcb", [64, 512], F32)
            wostg = mem.alloc("awostg", [128, 2, 512], F32)

            def ldwo(e):
                return e.dma_start(out=awo.ap(),
                                   in_=wT_scr["o"].rearrange("k p d -> p k d"))
            b.dma("qsca", ldwo, writes=[awo.k()], tag="aldo")

            for c in range(NCH):
                def ldq(e, c=c):
                    return e.dma_start(
                        out=qtc.ap(),
                        in_=q_scr[c].rearrange("j p t -> p j t"))
                b.dma("qsync", ldq, reads=[("qk_scr", "aqt", c)],
                      writes=[qtc.k()], tag=f"pldq{c}")
                for j in range(8):
                    def ldk(e, j=j):
                        return e.dma_start(
                            out=ktb.ap().rearrange("p (c t) -> p c t", t=512),
                            in_=k_scr[:, j].rearrange("c p t -> p c t"))
                    b.dma("qsync", ldk,
                          reads=[("qk_scr", "akt", cc) for cc in range(NCH)],
                          writes=[ktb.k()], tag=f"pldk{c}{j}")
                    for h2 in range(2):
                        h = 2 * j + h2
                        for pg in range(16):
                            b0 = (pg % 2) * 2
                            for ti in range(2):
                                tkt = pg * 2 + ti
                                def ms(e, j=j, h2=h2, tkt=tkt, ti=ti, b0=b0):
                                    return e.matmul(
                                        pA(b0 + ti),
                                        ktb.ap()[h2 * 64:(h2 + 1) * 64,
                                                 tkt * 128:(tkt + 1) * 128],
                                        qtc.ap()[h2 * 64:(h2 + 1) * 64, j, :],
                                        start=True, stop=True,
                                        tile_position=(h2 * 64, 0))
                                b.op("tensor", ms,
                                     reads=[ktb.k(), qtc.k(), ("psA", b0 + ti)],
                                     writes=[("psA", b0 + ti)], tag=f"pms{c}")
                            def pex(e, pg=pg, b0=b0):
                                return e.activation(
                                    pbuf.ap()[:, pg * 2:(pg + 1) * 2, :],
                                    psA[:, b0 * 512:(b0 + 2) * 512],
                                    AF.Exp, scale=P["cst"].ap()[:, 2:3])
                            b.op("scalar", pex,
                                 reads=[("psA", b0), ("psA", b0 + 1)],
                                 writes=[("pb", pg), ("pbq", pg // 4)],
                                 tag=f"pex{c}")
                        for tkt in range(32):
                            def mav(e, h=h, tkt=tkt):
                                return e.matmul(
                                    psB[0:65, 0:512],
                                    vbuf.ap()[:, tkt, h, :],
                                    pbuf.ap()[:, tkt, :],
                                    start=(tkt == 0), stop=(tkt == 31))
                            b.op("tensor", mav,
                                 reads=[("vb", tkt), ("vb1",),
                                        ("pbq", tkt // 8), ("psB", 0)],
                                 writes=[("psB", 0)], tag=f"pav{c}")
                        b.op("vector", lambda e: e.reciprocal(
                            rc1.ap(), psB[64:65, 0:512]),
                            reads=[("psB", 0)], writes=[rc1.k()], tag=f"prc{c}")
                        def brc(e):
                            return e.matmul(psB[0:64, 512:1024],
                                            P["ones1"].ap()[0:1, 0:64],
                                            rc1.ap(), start=True, stop=True)
                        b.op("tensor", brc,
                             reads=[rc1.k(), P["ones1"].k(), ("psB", 1)],
                             writes=[("psB", 1)], tag=f"pbrc{c}")
                        b.op("vector", lambda e: e.tensor_copy(
                            rcb.ap(), psB[0:64, 512:1024]),
                            reads=[("psB", 1)], writes=[rcb.k()], tag=f"prcb{c}")
                        def cmp_(e, j=j, h2=h2):
                            return e.tensor_mul(
                                aoT.ap()[h2 * 64:(h2 + 1) * 64, j, :],
                                psB[0:64, 0:512], rcb.ap())
                        b.op("vector", cmp_,
                             reads=[("psB", 0), rcb.k(), aoT.k()],
                             writes=[aoT.k()], tag=f"pcm{c}")
                for mt in range(4):
                    for nd in range(2):
                        bank = nd % 2
                        for kk in range(8):
                            def mo(e, mt=mt, nd=nd, kk=kk, bank=bank):
                                return e.matmul(
                                    pA(bank),
                                    aoT.ap()[:, kk, mt * 128:(mt + 1) * 128],
                                    awo.ap()[:, kk, nd * 512:(nd + 1) * 512],
                                    start=(kk == 0), stop=(kk == 7))
                            b.op("tensor", mo,
                                 reads=[aoT.k(), awo.k(), ("psA", bank)],
                                 writes=[("psA", bank)], tag=f"pmo{c}")
                        sp = (mt * 2 + nd) % 2
                        def oev(e, bank=bank, sp=sp):
                            return e.tensor_copy(wostg.ap()[:, sp, :], pA(bank))
                        b.op("vector", oev, reads=[("psA", bank)],
                             writes=[wostg.k(sp)], tag=f"poev{c}")
                        def oacc(e, c=c, mt=mt, nd=nd, sp=sp):
                            return e.dma_start(
                                out=x_state[c * CH + mt * 128:
                                            c * CH + (mt + 1) * 128,
                                            nd * 512:(nd + 1) * 512],
                                in_=wostg.ap()[:, sp, :], accum_op=ALU.add)
                        b.dma("qgp", oacc, reads=[wostg.k(sp)],
                              writes=[("xst", c, mt)], tag=f"poac{c}")
            mem.release(m0)

        if "setup" in phases:
            setup()
            b.barrier("p_setup")
        if "mamba" in phases:
            mamba()
            b.barrier("p_mamba")
        if "ffn1" in phases:
            ffn(0)
            b.barrier("p_ffn1")
        if "attn" in phases:
            attn()
            b.barrier("p_attn")
        if "ffn2" in phases:
            ffn(1)
            b.barrier("p_ffn2")
        if "ffn2" not in phases:
            b.dma("qsync", lambda e: e.dma_start(out=out, in_=x_state),
                  tag="cp_out")
        b.finish("gpsimd")
        b.emit(ctx)
    return nc


def kernel(**inputs):
    nc = build_program()
    in_maps = []
    full = {n: np.ascontiguousarray(inputs[n], dtype=np.float32)
            for n in INPUT_NAMES}
    cos_np, sin_np = _rope_tables()
    full["c_cos"] = cos_np
    full["c_sin"] = sin_np
    for core in range(8):
        m = dict(full)
        m["L0_attn_a_x"] = np.ascontiguousarray(full["L0_attn_a_x"][core])
        in_maps.append(m)
    res = run_bass_kernel_spmd(nc, in_maps, core_ids=list(range(8)))
    return np.stack([r["out"] for r in res.results]).astype(np.float32)
